# version 13
# baseline (speedup 1.0000x reference)
import math
from contextlib import ExitStack

import numpy as np

import concourse.bass as bass
import concourse.mybir as mybir
from concourse.bass_utils import run_bass_kernel_spmd

F32 = mybir.dt.float32
BF16 = mybir.dt.bfloat16
F8 = mybir.dt.float8e4
AF = mybir.ActivationFunctionType
ALU = mybir.AluOpType
AX = mybir.AxisListType.X

B = 2
S = 4096
DM = 2048
HID = 8192
NH = 8
NKV = 4
D = 128
VD = 256
TPC = 1024
LAMBDA_INIT = 0.8 - 0.6 * math.exp(-0.3 * 1)
EPS = 1e-5
SCALE = D ** -0.5
NEG = -30000.0

ENGINES = ("sync", "gpsimd", "vector", "scalar", "tensor")
DEBUG = False
_LAST_RES = None


class Ctr:
    def __init__(self, sem, step):
        self.sem = sem
        self.step = step
        self.n = 0


def build_program():
    nc = bass.Bass("TRN2", target_bir_lowering=False)

    xin = nc.dram_tensor("xin", [DM, TPC], F32, kind="ExternalInput")
    wqkv_l = nc.dram_tensor("wqkv_l", [24, 128, DM], F32, kind="ExternalInput")
    wv_r = nc.dram_tensor("wv_r", [16, 128, 1024], F32, kind="ExternalInput")
    outw_l = nc.dram_tensor("outw_l", [16, 128, DM], F32, kind="ExternalInput")
    wg_l = nc.dram_tensor("wg_l", [2, 64, 128, DM], F32, kind="ExternalInput")
    wl_l = nc.dram_tensor("wl_l", [16, 128, HID], F32, kind="ExternalInput")
    bq_cols = nc.dram_tensor("bq_cols", [128, 24], F32, kind="ExternalInput")
    bqv_row = nc.dram_tensor("bqv_row", [1, 1024], F32, kind="ExternalInput")
    outb_cols = nc.dram_tensor("outb_cols", [128, 16], F32, kind="ExternalInput")
    subln_cols = nc.dram_tensor("subln_cols", [128, 2], F32, kind="ExternalInput")
    lam_row = nc.dram_tensor("lam_row", [1, 512], F32, kind="ExternalInput")
    mask_in = nc.dram_tensor("mask_in", [128, 512], F32, kind="ExternalInput")
    iden_in = nc.dram_tensor("iden_in", [128, 128], F32, kind="ExternalInput")
    xout = nc.dram_tensor("xout", [DM, TPC], F32, kind="ExternalOutput")

    dbg = {}
    if DEBUG:
        dbg["d_y1"] = nc.dram_tensor("d_y1", [128, 16384], BF16, kind="ExternalOutput")
        dbg["d_x1"] = nc.dram_tensor("d_x1", [128, 16384], F32, kind="ExternalOutput")
        dbg["d_y2"] = nc.dram_tensor("d_y2", [128, 16384], BF16, kind="ExternalOutput")
        dbg["d_aa"] = nc.dram_tensor("d_aa", [128, 32768], BF16, kind="ExternalOutput")
        dbg["d_cst"] = nc.dram_tensor("d_cst", [128, 64], F32, kind="ExternalOutput")
        dbg["d_rows"] = nc.dram_tensor("d_rows", [1, 2576], F32, kind="ExternalOutput")
        dbg["d_rows0"] = nc.dram_tensor("d_rows0", [1, 2576], F32, kind="ExternalOutput")
        dbg["d_o"] = nc.dram_tensor("d_o", [128, 16384], BF16, kind="ExternalOutput")
        dbg["d_x2"] = nc.dram_tensor("d_x2", [128, 16384], F32, kind="ExternalOutput")

    ag_in = nc.dram_tensor("ag_in", [2048, 1024], BF16)
    ag_out = nc.dram_tensor("ag_out", [8192, 1024], BF16)
    nrm_in = nc.dram_tensor("nrm_in", [1, 64], F32)
    nrm_out = nc.dram_tensor("nrm_out", [4, 64], F32)

    with ExitStack() as ctx:
        sbt = lambda n, s, d: ctx.enter_context(nc.sbuf_tensor(n, s, d))
        pst = lambda n: ctx.enter_context(nc.psum_tensor(n, [128, 512], F32))
        mks = lambda n, st: Ctr(ctx.enter_context(nc.semaphore(n)), st)

        x_sb = sbt("x_sb", [128, 16 * 1024], F32)
        aa = sbt("aa", [128, 32 * 1024], BF16)
        y_sb = sbt("y_sb", [128, 16 * 1024], BF16)
        wb_sb = sbt("wb_sb", [128, 2 * 2048], BF16)
        wb8 = sbt("wb8", [128, 4 * 2048], F8)
        pt_sb = sbt("pt_sb", [128, 4 * 1024], BF16)
        kv_sb = sbt("kv_sb", [128, 4 * 512], BF16)
        mask_sb = sbt("mask_sb", [128, 512], BF16)
        iden = sbt("iden", [128, 128], BF16)
        tmp_a = sbt("tmp_a", [128, 1024], BF16)
        tmpf = sbt("tmpf", [128, 512], F32)
        cst = sbt("cst", [128, 64], F32)
        rows = sbt("rows", [1, 2576], F32)
        rowsb = sbt("rowsb", [1, 1024], BF16)
        onesr = sbt("onesr", [1, 512], BF16)
        onesrf = sbt("onesrf", [1, 128], F32)
        onesc = sbt("onesc", [128, 1], BF16)

        R_MEAN = 0
        R_T0 = 1024
        R_T1 = 2048
        R_QN = 2304
        R_KN = 2368
        R_KA = 2432
        R_C = 2496
        R_LAM = 2560
        A_Q = 0
        A_KM = 16 * 1024
        A_O = 16 * 1024

        a8 = aa[:, :].bitcast(F8)
        y8 = y_sb[:, :].bitcast(F8)
        y8_4 = y8[:, 0:16384].rearrange("p (ic tb j) -> p ic tb j", ic=16, tb=2, j=512)
        a8_4 = a8[:, 0:32768].rearrange("p (hh tb j) -> p hh tb j", hh=32, tb=2, j=512)
        DRM = mybir.MatmulPerfMode.DoubleRow

        pA, pB, pC, pD = pst("pA"), pst("pB"), pst("pC"), pst("pD")
        pE, pF, pG, pH = pst("pE"), pst("pF"), pst("pG"), pst("pH")

        dX = mks("dX", 16)
        dU8 = [mks("dU8_%d" % i, 16) for i in range(4)]
        dUb = [mks("dUb_%d" % i, 16) for i in range(2)]
        kvS = [mks("kv%d" % i, 16) for i in range(4)]
        tT = mks("tT", 1)
        tV = mks("tV", 1)
        tA = mks("tA", 1)
        tC = mks("tC", 1)

        prog = {e: [] for e in ENGINES}

        def W(eng, c, v=None):
            val = c.n if v is None else v
            if val <= 0:
                return
            prog[eng].append(lambda e, s=c.sem, vv=val: e.wait_ge(s, vv))

        def I(eng, meth, *args, inc=None, **kw):
            if inc is not None:
                inc.n += inc.step
            def f(e, meth=meth, args=args, kw=kw, inc=inc):
                ins = getattr(e, meth)(*args, **kw)
                if inc is not None:
                    ins.then_inc(inc.sem, inc.step)
            prog[eng].append(f)

        def barrier(engs=ENGINES):
            snap = [(c, c.n) for c in
                    (dX, dU8[0], dU8[1], dU8[2], dU8[3], dUb[0], dUb[1],
                     kvS[0], kvS[1], kvS[2], kvS[3], tT, tV, tA, tC)]
            for e in engs:
                for c, v in snap:
                    W(e, c, v)

        st8 = {"wrel": [0, 0, 0, 0], "i": 0}
        stb = {"wrel": [0, 0], "i": 0}

        def next_unit(srcs, dst=None):
            stt, nb, sems, tgt = \
                (st8, 4, dU8, wb8) if dst is wb8 else (stb, 2, dUb, wb_sb)
            wbuf = stt["i"] % nb
            stt["i"] += 1
            ctr = sems[wbuf]
            W("gpsimd", tT, stt["wrel"][wbuf])
            off = 0
            for src_ap, cols in srcs:
                I("gpsimd", "dma_start",
                  out=tgt[:, 2048 * wbuf + off:2048 * wbuf + off + cols],
                  in_=src_ap, inc=ctr)
                off += cols
            return wbuf, ctr, ctr.n

        def release_unit(wbuf, dst=None):
            (st8 if dst is wb8 else stb)["wrel"][wbuf] = tT.n

        I("sync", "dma_start",
          out=x_sb[:, :].rearrange("p (c t) -> p c t", c=16),
          in_=xin[:, :].rearrange("(c p) t -> p c t", p=128), inc=dX)
        I("gpsimd", "dma_start", out=mask_sb[:, :], in_=mask_in[:, :], inc=dX)
        I("gpsimd", "dma_start", out=iden[:, :], in_=iden_in[:, :], inc=dX)
        I("sync", "dma_start", out=cst[:, 24:48], in_=bq_cols[:, :], inc=dX)
        I("sync", "dma_start", out=cst[:, 48:64], in_=outb_cols[:, :], inc=dX)
        I("sync", "dma_start", out=cst[:, 16:18], in_=subln_cols[:, :], inc=dX)
        I("sync", "dma_start", out=rows[0:1, R_T0:R_T0 + 512], in_=lam_row[:, :], inc=dX)
        I("sync", "dma_start", out=rows[0:1, R_MEAN:R_MEAN + 1024], in_=bqv_row[:, :], inc=dX)
        ld0 = dX.n

        W("vector", dX, ld0)
        I("vector", "memset", onesr[:, :], 1.0)
        I("vector", "memset", onesrf[:, :], 1.0)
        I("vector", "memset", onesc[:, :], 1.0)
        I("vector", "tensor_copy", out=rowsb[0:1, 0:1024], in_=rows[0:1, R_MEAN:R_MEAN + 1024])
        I("vector", "tensor_scalar_mul", cst[:, 16:18], cst[:, 16:18], 1.0 - LAMBDA_INIT)
        I("vector", "tensor_mul", rows[0:1, R_T1:R_T1 + 128],
          rows[0:1, R_T0:R_T0 + 128], rows[0:1, R_T0 + 128:R_T0 + 256])
        I("vector", "tensor_mul", rows[0:1, R_T1 + 128:R_T1 + 256],
          rows[0:1, R_T0 + 256:R_T0 + 384], rows[0:1, R_T0 + 384:R_T0 + 512])
        I("vector", "reduce_sum", rows[0:1, R_LAM + 1:R_LAM + 2],
          rows[0:1, R_T1:R_T1 + 128], axis=AX)
        I("vector", "reduce_sum", rows[0:1, R_LAM + 2:R_LAM + 3],
          rows[0:1, R_T1 + 128:R_T1 + 256], axis=AX, inc=tV)
        W("scalar", tV)
        I("scalar", "activation", rows[0:1, R_LAM + 3:R_LAM + 5],
          rows[0:1, R_LAM + 1:R_LAM + 3], AF.Exp, inc=tA)
        W("vector", tA)
        I("vector", "tensor_scalar", out=rows[0:1, R_LAM:R_LAM + 1],
          in0=rows[0:1, R_LAM + 3:R_LAM + 4],
          scalar1=rows[0:1, R_LAM + 4:R_LAM + 5], scalar2=LAMBDA_INIT,
          op0=ALU.subtract, op1=ALU.add, inc=tV)
        barrier()

        def layernorm(to_fp8=False):
            mm_done = {}
            for ic in range(16):
                W("vector", tT, mm_done.get(ic - 1, 0))
                I("vector", "tensor_copy", out=tmp_a[:, :],
                  in_=x_sb[:, 1024 * ic:1024 * (ic + 1)], inc=tV)
                cast_done = tV.n
                W("tensor", tV, cast_done)
                I("tensor", "matmul", pE[0:1, :], onesc[:, 0:1], tmp_a[:, 0:512],
                  start=(ic == 0), stop=(ic == 15))
                I("tensor", "matmul", pF[0:1, :], onesc[:, 0:1], tmp_a[:, 512:1024],
                  start=(ic == 0), stop=(ic == 15), inc=tT)
                sum_done = tT.n
                W("scalar", tT, sum_done)
                I("scalar", "activation", tmp_a[:, :], tmp_a[:, :], AF.Square, inc=tA)
                W("tensor", tA)
                I("tensor", "matmul", pG[0:1, :], onesc[:, 0:1], tmp_a[:, 0:512],
                  start=(ic == 0), stop=(ic == 15))
                I("tensor", "matmul", pH[0:1, :], onesc[:, 0:1], tmp_a[:, 512:1024],
                  start=(ic == 0), stop=(ic == 15), inc=tT)
                mm_done[ic] = tT.n
            stats_done = tT.n
            W("vector", tT, stats_done)
            I("vector", "tensor_scalar_mul", rows[0:1, R_MEAN:R_MEAN + 512],
              pE[0:1, :], 1.0 / DM)
            I("vector", "tensor_scalar_mul", rows[0:1, R_MEAN + 512:R_MEAN + 1024],
              pF[0:1, :], 1.0 / DM)
            I("vector", "tensor_scalar_mul", rows[0:1, R_T0:R_T0 + 512],
              pG[0:1, :], 1.0 / DM)
            I("vector", "tensor_scalar_mul", rows[0:1, R_T0 + 512:R_T0 + 1024],
              pH[0:1, :], 1.0 / DM)
            for k in range(4):
                a, b = R_MEAN + 256 * k, R_T0 + 256 * k
                I("vector", "tensor_mul", rows[0:1, R_T1:R_T1 + 256],
                  rows[0:1, a:a + 256], rows[0:1, a:a + 256])
                I("vector", "tensor_sub", rows[0:1, b:b + 256],
                  rows[0:1, b:b + 256], rows[0:1, R_T1:R_T1 + 256])
            I("vector", "tensor_scalar_add", rows[0:1, R_T0:R_T0 + 1024],
              rows[0:1, R_T0:R_T0 + 1024], EPS)
            I("vector", "reciprocal", out=rows[0:1, R_T0:R_T0 + 1024],
              in_=rows[0:1, R_T0:R_T0 + 1024], inc=tV)
            W("scalar", tV)
            I("scalar", "activation", rows[0:1, R_T0:R_T0 + 1024],
              rows[0:1, R_T0:R_T0 + 1024], AF.Sqrt, inc=tA)
            for h in range(2):
                W("tensor", tA)
                W("tensor", tV)
                I("tensor", "matmul", pE[:, :], onesrf[0:1, :],
                  rows[0:1, R_MEAN + 512 * h:R_MEAN + 512 * (h + 1)], start=True, stop=True)
                I("tensor", "matmul", pF[:, :], onesrf[0:1, :],
                  rows[0:1, R_T0 + 512 * h:R_T0 + 512 * (h + 1)], start=True, stop=True,
                  inc=tT)
                ydst = y8 if to_fp8 else y_sb
                W("vector", tT)
                for ic in range(16):
                    I("vector", "tensor_sub", tmpf[:, 0:512],
                      x_sb[:, 1024 * ic + 512 * h:1024 * ic + 512 * (h + 1)], pE[:, :])
                    I("vector", "tensor_mul",
                      ydst[:, 1024 * ic + 512 * h:1024 * ic + 512 * (h + 1)],
                      tmpf[:, 0:512], pF[:, :],
                      inc=(tV if ic == 15 else None))

        def ffn():
            gel, mul, res = {}, {}, {}
            for half in range(2):
                for hh in range(32):
                    hc = 32 * half + hh
                    units = {}
                    for m in range(2):
                        wbuf, ctr, rdy = next_unit(
                            [(wg_l[m, hc, :, 0:1024], 1024),
                             (wg_l[m, hc, :, 1024:2048], 1024)], dst=wb8)
                        units[m] = (wbuf, rdy)
                        W("tensor", ctr, rdy)
                        for tb in range(2):
                            pp = (pA, pC)[tb] if m == 0 else (pB, pD)[tb]
                            if m == 0:
                                W("tensor", tA, gel.get((hc - 1, tb), 0))
                            else:
                                W("tensor", tV, mul.get((hc - 1, tb), 0))
                            if half == 1 and hc == 32:
                                W("tensor", tV)
                            for icp in range(8):
                                I("tensor", "matmul", pp[:, :],
                                  wb8[:, 2048 * wbuf + 256 * icp:2048 * wbuf + 256 * (icp + 1)]
                                  .rearrange("p (i o) -> p i o", i=2),
                                  y8_4[:, 2 * icp:2 * icp + 2, tb, :],
                                  start=(icp == 0), stop=(icp == 7), perf_mode=DRM,
                                  inc=(tT if icp == 7 else None))
                            if m == 0:
                                gel[("mmg", hc, tb)] = tT.n
                            else:
                                mul[("mml", hc, tb)] = tT.n
                        release_unit(wbuf, wb8)
                    for tb in range(2):
                        pg = (pA, pC)[tb]
                        pl = (pB, pD)[tb]
                        W("scalar", tT, gel[("mmg", hc, tb)])
                        W("scalar", tV, mul.get((hc - 1, tb), 0))
                        I("scalar", "activation", tmp_a[:, 512 * tb:512 * (tb + 1)],
                          pg[:, :], AF.Gelu_apprx_tanh, inc=tA)
                        gel[(hc, tb)] = tA.n
                        W("vector", tT, mul[("mml", hc, tb)])
                        W("vector", tA, gel[(hc, tb)])
                        I("vector", "tensor_mul",
                          a8[:, 1024 * hh + 512 * tb:1024 * hh + 512 * (tb + 1)],
                          tmp_a[:, 512 * tb:512 * (tb + 1)], pl[:, :], inc=tV)
                        mul[(hc, tb)] = tV.n
                for oc in range(16):
                    if oc == 0:
                        W("tensor", tV)
                    for ku in range(2):
                        wbuf, ctr, rdy = next_unit(
                            [(wl_l[oc, :, 4096 * half + 2048 * ku + 1024 * k:
                                   4096 * half + 2048 * ku + 1024 * (k + 1)], 1024)
                             for k in range(2)], dst=wb8)
                        W("tensor", ctr, rdy)
                        for tb in range(2):
                            pd = (pA, pC)[tb] if oc % 2 == 0 else (pB, pD)[tb]
                            if ku == 0:
                                W("tensor", tV, res.get((oc - 2, tb, half), 0))
                            for hhp in range(8):
                                I("tensor", "matmul", pd[:, :],
                                  wb8[:, 2048 * wbuf + 256 * hhp:2048 * wbuf + 256 * (hhp + 1)]
                                  .rearrange("p (i o) -> p i o", i=2),
                                  a8_4[:, 16 * ku + 2 * hhp:16 * ku + 2 * hhp + 2, tb, :],
                                  start=(ku == 0 and hhp == 0), stop=(ku == 1 and hhp == 7),
                                  perf_mode=DRM,
                                  inc=(tT if hhp == 7 else None))
                            if ku == 1:
                                W("vector", tT)
                                I("vector", "scalar_tensor_tensor",
                                  out=x_sb[:, 1024 * oc + 512 * tb:1024 * oc + 512 * (tb + 1)],
                                  in0=pd[:, :], scalar=0.5,
                                  in1=x_sb[:, 1024 * oc + 512 * tb:1024 * oc + 512 * (tb + 1)],
                                  op0=ALU.mult, op1=ALU.add, inc=tV)
                                res[(oc, tb, half)] = tV.n
                        release_unit(wbuf, wb8)
            return res

        def dump(name, src):
            if DEBUG:
                I("sync", "dma_start", out=dbg[name][:, :], in_=src, inc=dX)

        if DEBUG:
            I("sync", "dma_start", out=dbg["d_rows0"][:, :], in_=rows[:, :], inc=dX)

        layernorm(to_fp8=True)
        barrier()
        ffn()
        barrier()
        dump("d_x1", x_sb[:, :])
        layernorm()
        barrier()
        dump("d_y2", y_sb[:, :])

        qkev = {}
        for oc in range(24):
            wbuf, ctr, rdy = next_unit([(wqkv_l[oc, :, 0:1024], 1024),
                                        (wqkv_l[oc, :, 1024:2048], 1024)])
            W("tensor", ctr, rdy)
            for tb in range(2):
                pq = (pA, pC)[tb] if oc % 2 == 0 else (pB, pD)[tb]
                W("tensor", tV, qkev.get((oc - 2, tb), 0))
                for ic in range(16):
                    I("tensor", "matmul", pq[:, :],
                      wb_sb[:, 2048 * wbuf + 128 * ic:2048 * wbuf + 128 * (ic + 1)],
                      y_sb[:, 1024 * ic + 512 * tb:1024 * ic + 512 * (tb + 1)],
                      start=(ic == 0), stop=(ic == 15),
                      inc=(tT if ic == 15 else None))
                W("vector", tT)
                dst = (A_Q + 1024 * oc if oc < 16 else A_KM + 1024 * (oc - 16)) + 512 * tb
                I("vector", "tensor_scalar_add", aa[:, dst:dst + 512], pq[:, :],
                  cst[:, 24 + oc:25 + oc], inc=tV)
                qkev[(oc, tb)] = tV.n
            release_unit(wbuf)

        vtev = {}
        W("tensor", tV)
        for jg in range(4):
            for ic in range(16):
                wbuf, ctr, rdy = next_unit([(wv_r[ic, :, :], 1024)])
                W("tensor", ctr, rdy)
                for jj in range(2):
                    j = 2 * jg + jj
                    for vh in range(2):
                        pv = (pA, pB, pC, pD)[2 * jj + vh]
                        if ic == 0:
                            W("tensor", tA, vtev.get((jg - 1, jj, vh), 0))
                            I("tensor", "matmul", pv[:, :], onesr[0:1, 0:128],
                              rowsb[0:1, 512 * vh:512 * (vh + 1)], start=True, stop=False)
                        I("tensor", "matmul", pv[:, :],
                          y_sb[:, 1024 * ic + 128 * j:1024 * ic + 128 * (j + 1)],
                          wb_sb[:, 2048 * wbuf + 512 * vh:2048 * wbuf + 512 * (vh + 1)],
                          start=False, stop=(ic == 15),
                          inc=(tT if (ic == 15 or (jj == 1 and vh == 1)) else None))
                        if ic == 15:
                            W("scalar", tT)
                            dst = A_KM + 8192 + 1024 * j + 512 * vh
                            I("scalar", "activation", aa[:, dst:dst + 512], pv[:, :],
                              AF.Copy, inc=tA)
                            vtev[(jg, jj, vh)] = tA.n
                release_unit(wbuf)

        def norms_ok(base_off, n_chunks, dst_off):
            nred = {}
            for c in range(n_chunks):
                W("scalar", tV, nred.get(c - 1, 0))
                if c == 0:
                    W("scalar", tV)
                src = base_off + 1024 * c
                I("scalar", "activation", tmp_a[:, 0:1024], aa[:, src:src + 1024],
                  AF.Square, inc=tA)
                W("tensor", tA)
                W("tensor", tV, nred.get(c - 1, 0))
                pe = (pE, pF)[c % 2]
                I("tensor", "matmul", pe[0:1, :], onesc[:, 0:1], tmp_a[:, 0:512],
                  start=True, stop=True)
                I("tensor", "matmul", pe[32:33, :], onesc[:, 0:1], tmp_a[:, 512:1024],
                  start=True, stop=True, inc=tT)
                W("vector", tT)
                I("vector", "reduce_max", rows[0:1, R_T1:R_T1 + 1], pe[0:1, :], axis=AX)
                I("vector", "reduce_max", rows[0:1, R_T1 + 1:R_T1 + 2], pe[32:33, :], axis=AX)
                I("vector", "tensor_max", rows[0:1, dst_off + c:dst_off + c + 1],
                  rows[0:1, R_T1:R_T1 + 1], rows[0:1, R_T1 + 1:R_T1 + 2], inc=tV)
                nred[c] = tV.n

        norms_ok(A_Q, 16, R_QN)
        norms_ok(A_KM, 8, R_KN)

        W("sync", tA)
        W("sync", tV)
        I("sync", "dma_start",
          out=ag_in[:, :].rearrange("(u p) t -> p u t", p=128),
          in_=aa[:, A_KM:A_KM + 16384].rearrange("p (u t) -> p u t", u=16), inc=dX)
        I("sync", "dma_start", out=nrm_in[:, :], in_=rows[0:1, R_KN:R_KN + 64], inc=dX)
        W("gpsimd", dX)
        I("gpsimd", "collective_compute", "AllGather", ALU.bypass,
          replica_groups=[[0, 1, 2, 3], [4, 5, 6, 7]],
          ins=[ag_in[:, :]], outs=[ag_out[:, :]], inc=tC)
        I("gpsimd", "collective_compute", "AllGather", ALU.bypass,
          replica_groups=[[0, 1, 2, 3], [4, 5, 6, 7]],
          ins=[nrm_in[:, :]], outs=[nrm_out[:, :]], inc=tC)
        W("sync", tC)
        for r in range(4):
            I("sync", "dma_start", out=rows[0:1, R_T1 + 64 * r:R_T1 + 64 * (r + 1)],
              in_=nrm_out[r:r + 1, :], inc=dX)
        W("vector", dX)
        I("vector", "tensor_max", rows[0:1, R_T1:R_T1 + 64],
          rows[0:1, R_T1:R_T1 + 64], rows[0:1, R_T1 + 64:R_T1 + 128])
        I("vector", "tensor_max", rows[0:1, R_T1 + 128:R_T1 + 192],
          rows[0:1, R_T1 + 128:R_T1 + 192], rows[0:1, R_T1 + 192:R_T1 + 256])
        I("vector", "tensor_max", rows[0:1, R_KA:R_KA + 64],
          rows[0:1, R_T1:R_T1 + 64], rows[0:1, R_T1 + 128:R_T1 + 192])
        for h in range(NH):
            I("vector", "tensor_copy", out=rows[0:1, R_C + 16 + 2 * h:R_C + 18 + 2 * h],
              in_=rows[0:1, R_KA + 2 * (h // 2):R_KA + 2 * (h // 2) + 2])
        I("vector", "tensor_add", rows[0:1, R_C:R_C + 16],
          rows[0:1, R_QN:R_QN + 16], rows[0:1, R_C + 16:R_C + 32])
        I("vector", "tensor_scalar_mul", rows[0:1, R_C:R_C + 16],
          rows[0:1, R_C:R_C + 16], -0.5 * SCALE, inc=tV)
        W("tensor", tV)
        I("tensor", "matmul", pH[:, 0:16], onesrf[0:1, :], rows[0:1, R_C:R_C + 16],
          start=True, stop=True, inc=tT)
        W("scalar", tT)
        I("scalar", "copy", out=cst[:, 0:16], in_=pH[:, 0:16], inc=tA)
        barrier()
        dump("d_aa", aa[:, :])
        dump("d_cst", cst[:, :])
        if DEBUG:
            I("sync", "dma_start", out=dbg["d_rows"][:, :], in_=rows[:, :], inc=dX)

        kv_rel = [0, 0, 0, 0]
        pt_rel = [0, 0, 0, 0]

        def pv_group(tp, c0p, nt, final):
            bufk = tp % 4
            bufp = tp % 4
            for m in range(2):
                for vh in range(2):
                    po = (pA, pB, pC, pD)[2 * m + vh]
                    I("tensor", "matmul", po[:, c0p:512],
                      kv_sb[:, 512 * bufk + 256 + 128 * vh:512 * bufk + 256 + 128 * (vh + 1)],
                      pt_sb[:, 1024 * bufp + 512 * m + c0p:1024 * bufp + 512 * m + 512],
                      start=(tp == 0), stop=(tp == nt - 1))
                pg_s = (pG, pH)[m]
                I("tensor", "matmul", pg_s[0:1, c0p:512], onesc[:, 0:1],
                  pt_sb[:, 1024 * bufp + 512 * m + c0p:1024 * bufp + 512 * m + 512],
                  start=(tp == 0), stop=(tp == nt - 1),
                  inc=(tT if m == 1 else None))
            kv_rel[bufk] = tT.n
            pt_rel[bufp] = tT.n

        def attn_head(gdh, h, qh):
            nt = 16 if qh == 0 else 32
            exp_done = {}
            prev = None
            for t in range(nt):
                jmin = max(t // 4 - 4 * qh, 0)
                c0 = 128 * jmin
                buf = t % 4
                ptb = t % 4
                rt, jt = t % 4, t // 4
                W("sync", tT, kv_rel[buf])
                rbase = 2048 * rt
                I("sync", "dma_start", out=kv_sb[:, 512 * buf:512 * buf + 128],
                  in_=ag_out[rbase + 256 * gdh:rbase + 256 * gdh + 128,
                             128 * jt:128 * (jt + 1)], inc=kvS[buf])
                I("sync", "dma_start", out=kv_sb[:, 512 * buf + 128:512 * buf + 256],
                  in_=ag_out[rbase + 256 * gdh + 128:rbase + 256 * gdh + 256,
                             128 * jt:128 * (jt + 1)], inc=kvS[buf])
                W("gpsimd", tT, kv_rel[buf])
                I("gpsimd", "dma_start", out=kv_sb[:, 512 * buf + 256:512 * buf + 512],
                  in_=ag_out[rbase + 1024 + 128 * jt:rbase + 1024 + 128 * (jt + 1),
                             256 * gdh:256 * (gdh + 1)], inc=kvS[buf])

                W("tensor", kvS[buf])
                masked = (qh == 0) or (t >= 16)
                for m in range(2):
                    ps = (pE, pF)[m]
                    qoff = A_Q + 1024 * (2 * h + m) + 512 * qh
                    if t >= 1:
                        W("tensor", tA, exp_done[(t - 1, m)])
                    I("tensor", "matmul", ps[:, c0:512],
                      kv_sb[:, 512 * buf + 128 * m:512 * buf + 128 * (m + 1)],
                      aa[:, qoff + c0:qoff + 512],
                      start=True, stop=(not masked),
                      inc=(tT if (m == 1 and not masked) else None))
                if masked:
                    for m in range(2):
                        ps = (pE, pF)[m]
                        I("tensor", "matmul", ps[:, c0:c0 + 128], iden[:, :],
                          mask_sb[:, 128 * rt:128 * (rt + 1)],
                          start=False, stop=True,
                          inc=(tT if m == 1 else None))
                scores_done = tT.n

                if prev is not None:
                    pv_group(prev[0], prev[1], nt, False)

                W("scalar", tT, scores_done)
                W("scalar", tT, pt_rel[ptb])
                for m in range(2):
                    ps = (pE, pF)[m]
                    I("scalar", "activation",
                      pt_sb[:, 1024 * ptb + 512 * m + c0:1024 * ptb + 512 * m + 512],
                      ps[:, c0:512], AF.Exp,
                      bias=cst[:, 2 * h + m:2 * h + m + 1], scale=SCALE,
                      inc=tA)
                    exp_done[(t, m)] = tA.n
                prev = (t, c0)

            W("tensor", tA, exp_done[(nt - 1, 1)])
            pv_group(prev[0], prev[1], nt, True)
            pv_done = tT.n

            W("vector", tT, pv_done)
            I("vector", "reciprocal", out=rows[0:1, R_T0:R_T0 + 512], in_=pG[0:1, 0:512])
            I("vector", "reciprocal", out=rows[0:1, R_T0 + 512:R_T0 + 1024],
              in_=pH[0:1, 0:512])
            I("vector", "tensor_scalar", out=rows[0:1, R_T0 + 512:R_T0 + 1024],
              in0=rows[0:1, R_T0 + 512:R_T0 + 1024],
              scalar1=rows[0:1, R_LAM:R_LAM + 1], scalar2=None, op0=ALU.mult, inc=tV)
            rdone = tV.n
            W("tensor", tV, rdone)
            I("tensor", "matmul", pG[:, 0:512], onesrf[0:1, :],
              rows[0:1, R_T0:R_T0 + 512], start=True, stop=True, inc=tT)
            W("scalar", tT)
            I("scalar", "copy", out=tmpf[:, :], in_=pG[:, :], inc=tA)
            W("vector", tA)
            for vh in range(2):
                dst = A_O + 1024 * (2 * h + vh) + 512 * qh
                I("vector", "tensor_mul", aa[:, dst:dst + 512],
                  (pA, pB)[vh][:, :], tmpf[:, :], inc=(tV if vh == 1 else None))
            prod1_done = tV.n
            W("tensor", tV, prod1_done)
            I("tensor", "matmul", pH[:, 0:512], onesrf[0:1, :],
              rows[0:1, R_T0 + 512:R_T0 + 1024], start=True, stop=True, inc=tT)
            W("scalar", tT)
            I("scalar", "copy", out=tmpf[:, :], in_=pH[:, :], inc=tA)
            W("vector", tA)
            for vh in range(2):
                pm = (pC, pD)[vh]
                dst = A_O + 1024 * (2 * h + vh) + 512 * qh
                I("vector", "tensor_mul", pm[:, :], pm[:, :], tmpf[:, :])
                I("vector", "tensor_sub", aa[:, dst:dst + 512],
                  aa[:, dst:dst + 512], pm[:, :], inc=(tV if vh == 1 else None))
            comb_done = tV.n
            W("scalar", tV, comb_done)
            for vh in range(2):
                dst = A_O + 1024 * (2 * h + vh) + 512 * qh
                I("scalar", "activation", tmp_a[:, 512 * vh:512 * (vh + 1)],
                  aa[:, dst:dst + 512], AF.Square, inc=(tA if vh == 1 else None))
            W("tensor", tA)
            for vh in range(2):
                I("tensor", "matmul", pG[0:1, 0:512], onesc[:, 0:1],
                  tmp_a[:, 512 * vh:512 * (vh + 1)], start=(vh == 0), stop=(vh == 1),
                  inc=(tT if vh == 1 else None))
            W("vector", tT)
            I("vector", "tensor_scalar", out=rows[0:1, R_T0:R_T0 + 512],
              in0=pG[0:1, 0:512], scalar1=1.0 / VD, scalar2=EPS,
              op0=ALU.mult, op1=ALU.add)
            I("vector", "reciprocal", out=rows[0:1, R_T0:R_T0 + 512],
              in_=rows[0:1, R_T0:R_T0 + 512], inc=tV)
            W("scalar", tV)
            I("scalar", "activation", rows[0:1, R_T0 + 512:R_T0 + 1024],
              rows[0:1, R_T0:R_T0 + 512], AF.Sqrt, inc=tA)
            W("tensor", tA)
            I("tensor", "matmul", pH[:, 0:512], onesrf[0:1, :],
              rows[0:1, R_T0 + 512:R_T0 + 1024], start=True, stop=True, inc=tT)
            W("scalar", tT)
            I("scalar", "copy", out=tmpf[:, :], in_=pH[:, :], inc=tA)
            W("vector", tA)
            for vh in range(2):
                dst = A_O + 1024 * (2 * h + vh) + 512 * qh
                I("vector", "scalar_tensor_tensor",
                  out=aa[:, dst:dst + 512], in0=aa[:, dst:dst + 512],
                  scalar=cst[:, 16 + vh:17 + vh], in1=tmpf[:, :],
                  op0=ALU.mult, op1=ALU.mult, inc=tV)

        for gdh in range(NKV):
            for hh2 in range(2):
                h = 2 * gdh + hh2
                for qh in range(2):
                    attn_head(gdh, h, qh)
                    barrier(("tensor", "vector", "scalar"))

        dump("d_o", aa[:, A_O:A_O + 16384])

        ores = {}
        for oc in range(16):
            wbuf, ctr, rdy = next_unit([(outw_l[oc, :, 0:1024], 1024),
                                        (outw_l[oc, :, 1024:2048], 1024)])
            W("tensor", ctr, rdy)
            for tb in range(2):
                po = (pA, pC)[tb] if oc % 2 == 0 else (pB, pD)[tb]
                W("tensor", tV, ores.get((oc - 2, tb), 0))
                for ic in range(16):
                    I("tensor", "matmul", po[:, :],
                      wb_sb[:, 2048 * wbuf + 128 * ic:2048 * wbuf + 128 * (ic + 1)],
                      aa[:, A_O + 1024 * ic + 512 * tb:A_O + 1024 * ic + 512 * (tb + 1)],
                      start=(ic == 0), stop=(ic == 15),
                      inc=(tT if ic == 15 else None))
                W("vector", tT)
                I("vector", "scalar_tensor_tensor",
                  out=x_sb[:, 1024 * oc + 512 * tb:1024 * oc + 512 * (tb + 1)],
                  in0=po[:, :], scalar=cst[:, 48 + oc:49 + oc],
                  in1=x_sb[:, 1024 * oc + 512 * tb:1024 * oc + 512 * (tb + 1)],
                  op0=ALU.add, op1=ALU.add, inc=tV)
                ores[(oc, tb)] = tV.n
            release_unit(wbuf)
        barrier()
        dump("d_x2", x_sb[:, :])

        layernorm(to_fp8=True)
        barrier()
        res3 = ffn()

        for oc in range(16):
            W("sync", tV, res3[(oc, 1, 1)])
            I("sync", "dma_start",
              out=xout[128 * oc:128 * (oc + 1), :],
              in_=x_sb[:, 1024 * oc:1024 * (oc + 1)], inc=dX)
        W("sync", dX)

        block = ctx.enter_context(nc.Block())

        @block.sync
        def _(e):
            for f in prog["sync"]:
                f(e)

        @block.gpsimd
        def _(e):
            for f in prog["gpsimd"]:
                f(e)

        @block.vector
        def _(e):
            for f in prog["vector"]:
                f(e)

        @block.scalar
        def _(e):
            for f in prog["scalar"]:
                f(e)

        @block.tensor
        def _(e):
            for f in prog["tensor"]:
                f(e)

    return nc


_NC_CACHE = None


def kernel(L0_attn_hidden_states, L0_attn_Wqkv_w, L0_attn_Wqkv_b,
           L0_attn_lambda_q1, L0_attn_lambda_k1, L0_attn_lambda_q2,
           L0_attn_lambda_k2, L0_attn_subln_w, L0_attn_out_w, L0_attn_out_b,
           L0_ffn_w_gating, L0_ffn_w_linear):
    global _NC_CACHE
    x0 = np.asarray(L0_attn_hidden_states, dtype=np.float32)
    Wqkv = np.asarray(L0_attn_Wqkv_w, dtype=np.float32)
    bqkv = np.asarray(L0_attn_Wqkv_b, dtype=np.float32)
    outw = np.asarray(L0_attn_out_w, dtype=np.float32)
    outb = np.asarray(L0_attn_out_b, dtype=np.float32)
    wg = np.asarray(L0_ffn_w_gating, dtype=np.float32)
    wl = np.asarray(L0_ffn_w_linear, dtype=np.float32)
    subln = np.asarray(L0_attn_subln_w, dtype=np.float32)

    wqkv_l = np.ascontiguousarray(
        Wqkv[:3072].reshape(24, 128, 16, 128).transpose(0, 3, 2, 1).reshape(24, 128, 2048))
    wv_r = np.ascontiguousarray(Wqkv[3072:].reshape(1024, 16, 128).transpose(1, 2, 0))
    outw_l = np.ascontiguousarray(
        outw.reshape(16, 128, 16, 128).transpose(0, 3, 2, 1).reshape(16, 128, 2048))
    wg_l = np.ascontiguousarray(
        wg.reshape(2, 16, 128, 64, 128).transpose(0, 3, 2, 1, 4).reshape(2, 64, 128, 2048))
    wl_l = np.ascontiguousarray(
        wl.reshape(64, 128, 16, 128).transpose(2, 1, 0, 3).reshape(16, 128, 8192))
    bq_cols = np.ascontiguousarray(bqkv[:3072].reshape(24, 128).T)
    bqv_row = np.ascontiguousarray(bqkv[3072:].reshape(1, 1024))
    outb_cols = np.ascontiguousarray(outb.reshape(16, 128).T)
    subln_cols = np.ascontiguousarray(subln.reshape(2, 128).T)
    lam_row = np.ascontiguousarray(np.concatenate(
        [np.asarray(L0_attn_lambda_q1, dtype=np.float32),
         np.asarray(L0_attn_lambda_k1, dtype=np.float32),
         np.asarray(L0_attn_lambda_q2, dtype=np.float32),
         np.asarray(L0_attn_lambda_k2, dtype=np.float32)]).reshape(1, 512))

    shared = dict(wqkv_l=wqkv_l, wv_r=wv_r, outw_l=outw_l, wg_l=wg_l, wl_l=wl_l,
                  bq_cols=bq_cols, bqv_row=bqv_row, outb_cols=outb_cols,
                  subln_cols=subln_cols, lam_row=lam_row)

    tok_idx = {}
    in_maps = []
    kl = np.arange(128)[:, None]
    ql = np.arange(128)[None, :]
    for c in range(8):
        b, r = c // 4, c % 4
        idx = np.concatenate([np.arange(128 * (4 * j + r), 128 * (4 * j + r) + 128)
                              for j in range(8)])
        tok_idx[c] = (b, idx)
        mask = np.zeros((128, 512), dtype=np.float32)
        for rr in range(4):
            mask[:, 128 * rr:128 * (rr + 1)] = np.where(
                128 * rr + kl > 128 * r + ql, NEG, 0.0)
        m = dict(shared)
        m["xin"] = np.ascontiguousarray(x0[b, idx, :].T)
        m["mask_in"] = mask
        m["iden_in"] = np.eye(128, dtype=np.float32)
        in_maps.append(m)

    if _NC_CACHE is None:
        _NC_CACHE = build_program()
    res = run_bass_kernel_spmd(_NC_CACHE, in_maps, core_ids=list(range(8)))
    globals()["_LAST_RES"] = res

    out = np.empty((B, S, DM), dtype=np.float32)
    for c in range(8):
        b, idx = tok_idx[c]
        out[b, idx, :] = res.results[c]["xout"].T
    return out


if __name__ == "__main__":
    import reference
    inputs = reference.setup_inputs(0)
    inputs = {k: np.asarray(v) for k, v in inputs.items()}
    got = kernel(**inputs)
    exp = np.asarray(reference.reference(**inputs))
    err = np.mean((got - exp) ** 2) / np.mean(exp ** 2)
    print("residual variance ratio:", err)
    print("max abs diff:", np.abs(got - exp).max())


# version 14
# speedup vs baseline: 1.0328x; 1.0328x over previous
import math
from contextlib import ExitStack

import numpy as np

import concourse.bass as bass
import concourse.mybir as mybir
from concourse.bass_utils import run_bass_kernel_spmd

F32 = mybir.dt.float32
BF16 = mybir.dt.bfloat16
F8 = mybir.dt.float8e4
AF = mybir.ActivationFunctionType
ALU = mybir.AluOpType
AX = mybir.AxisListType.X

B = 2
S = 4096
DM = 2048
HID = 8192
NH = 8
NKV = 4
D = 128
VD = 256
TPC = 1024
LAMBDA_INIT = 0.8 - 0.6 * math.exp(-0.3 * 1)
EPS = 1e-5
SCALE = D ** -0.5
NEG = -30000.0

ENGINES = ("sync", "gpsimd", "vector", "scalar", "tensor")
DEBUG = False
_LAST_RES = None


class Ctr:
    def __init__(self, sem, step):
        self.sem = sem
        self.step = step
        self.n = 0


def build_program():
    nc = bass.Bass("TRN2", target_bir_lowering=False)

    xin = nc.dram_tensor("xin", [DM, TPC], F32, kind="ExternalInput")
    wqkv_l = nc.dram_tensor("wqkv_l", [24, 128, DM], F32, kind="ExternalInput")
    wv_r = nc.dram_tensor("wv_r", [16, 128, 1024], F32, kind="ExternalInput")
    outw_l = nc.dram_tensor("outw_l", [16, 128, DM], F32, kind="ExternalInput")
    wg_l = nc.dram_tensor("wg_l", [2, 64, 128, DM], F32, kind="ExternalInput")
    wl_l = nc.dram_tensor("wl_l", [16, 128, HID], F32, kind="ExternalInput")
    bq_cols = nc.dram_tensor("bq_cols", [128, 24], F32, kind="ExternalInput")
    bqv_row = nc.dram_tensor("bqv_row", [1, 1024], F32, kind="ExternalInput")
    outb_cols = nc.dram_tensor("outb_cols", [128, 16], F32, kind="ExternalInput")
    subln_cols = nc.dram_tensor("subln_cols", [128, 2], F32, kind="ExternalInput")
    lam_row = nc.dram_tensor("lam_row", [1, 512], F32, kind="ExternalInput")
    mask_in = nc.dram_tensor("mask_in", [128, 512], F32, kind="ExternalInput")
    iden_in = nc.dram_tensor("iden_in", [128, 128], F32, kind="ExternalInput")
    xout = nc.dram_tensor("xout", [DM, TPC], F32, kind="ExternalOutput")

    dbg = {}
    if DEBUG:
        dbg["d_y1"] = nc.dram_tensor("d_y1", [128, 16384], BF16, kind="ExternalOutput")
        dbg["d_x1"] = nc.dram_tensor("d_x1", [128, 16384], F32, kind="ExternalOutput")
        dbg["d_y2"] = nc.dram_tensor("d_y2", [128, 16384], BF16, kind="ExternalOutput")
        dbg["d_aa"] = nc.dram_tensor("d_aa", [128, 32768], BF16, kind="ExternalOutput")
        dbg["d_cst"] = nc.dram_tensor("d_cst", [128, 64], F32, kind="ExternalOutput")
        dbg["d_rows"] = nc.dram_tensor("d_rows", [1, 2576], F32, kind="ExternalOutput")
        dbg["d_rows0"] = nc.dram_tensor("d_rows0", [1, 2576], F32, kind="ExternalOutput")
        dbg["d_o"] = nc.dram_tensor("d_o", [128, 16384], BF16, kind="ExternalOutput")
        dbg["d_x2"] = nc.dram_tensor("d_x2", [128, 16384], F32, kind="ExternalOutput")

    ag_in = nc.dram_tensor("ag_in", [2048, 1024], BF16)
    ag_out = nc.dram_tensor("ag_out", [8192, 1024], BF16)
    nrm_in = nc.dram_tensor("nrm_in", [1, 64], F32)
    nrm_out = nc.dram_tensor("nrm_out", [4, 64], F32)

    with ExitStack() as ctx:
        sbt = lambda n, s, d: ctx.enter_context(nc.sbuf_tensor(n, s, d))
        pst = lambda n: ctx.enter_context(nc.psum_tensor(n, [128, 512], F32))
        mks = lambda n, st: Ctr(ctx.enter_context(nc.semaphore(n)), st)

        x_sb = sbt("x_sb", [128, 16 * 1024], F32)
        aa = sbt("aa", [128, 32 * 1024], BF16)
        y_sb = sbt("y_sb", [128, 16 * 1024], BF16)
        wb_sb = sbt("wb_sb", [128, 2 * 2048], BF16)
        wb8 = sbt("wb8", [128, 4 * 2048], F8)
        pt_sb = sbt("pt_sb", [128, 4 * 1024], BF16)
        kv_sb = sbt("kv_sb", [128, 4 * 512], BF16)
        mask_sb = sbt("mask_sb", [128, 512], BF16)
        iden = sbt("iden", [128, 128], BF16)
        tmp_a = sbt("tmp_a", [128, 1024], BF16)
        tmpf = sbt("tmpf", [128, 512], F32)
        cst = sbt("cst", [128, 64], F32)
        rows = sbt("rows", [1, 2576], F32)
        rowsb = sbt("rowsb", [1, 1024], BF16)
        onesr = sbt("onesr", [1, 512], BF16)
        onesrf = sbt("onesrf", [1, 128], F32)
        onesc = sbt("onesc", [128, 1], BF16)

        R_MEAN = 0
        R_T0 = 1024
        R_T1 = 2048
        R_QN = 2304
        R_KN = 2368
        R_KA = 2432
        R_C = 2496
        R_LAM = 2560
        A_Q = 0
        A_KM = 16 * 1024
        A_O = 16 * 1024

        a8 = aa[:, :].bitcast(F8)
        y8 = y_sb[:, :].bitcast(F8)
        y8_4 = y8[:, 0:16384].rearrange("p (ic tb j) -> p ic tb j", ic=16, tb=2, j=512)
        a8_4 = a8[:, 0:32768].rearrange("p (hh tb j) -> p hh tb j", hh=32, tb=2, j=512)
        DRM = mybir.MatmulPerfMode.DoubleRow

        pA, pB, pC, pD = pst("pA"), pst("pB"), pst("pC"), pst("pD")
        pE, pF, pG, pH = pst("pE"), pst("pF"), pst("pG"), pst("pH")

        dX = mks("dX", 16)
        dU8 = [mks("dU8_%d" % i, 16) for i in range(4)]
        dUb = [mks("dUb_%d" % i, 16) for i in range(2)]
        kvS = [mks("kv%d" % i, 16) for i in range(4)]
        tT = mks("tT", 1)
        tV = mks("tV", 1)
        tA = mks("tA", 1)
        tC = mks("tC", 1)

        prog = {e: [] for e in ENGINES}

        def W(eng, c, v=None):
            val = c.n if v is None else v
            if val <= 0:
                return
            prog[eng].append(lambda e, s=c.sem, vv=val: e.wait_ge(s, vv))

        def I(eng, meth, *args, inc=None, **kw):
            if inc is not None:
                inc.n += inc.step
            def f(e, meth=meth, args=args, kw=kw, inc=inc):
                ins = getattr(e, meth)(*args, **kw)
                if inc is not None:
                    ins.then_inc(inc.sem, inc.step)
            prog[eng].append(f)

        def barrier(engs=ENGINES):
            snap = [(c, c.n) for c in
                    (dX, dU8[0], dU8[1], dU8[2], dU8[3], dUb[0], dUb[1],
                     kvS[0], kvS[1], kvS[2], kvS[3], tT, tV, tA, tC)]
            for e in engs:
                for c, v in snap:
                    W(e, c, v)

        st8 = {"wrel": [0, 0, 0, 0], "i": 0}
        stb = {"wrel": [0, 0], "i": 0}

        def next_unit(srcs, dst=None):
            stt, nb, sems, tgt = \
                (st8, 4, dU8, wb8) if dst is wb8 else (stb, 2, dUb, wb_sb)
            wbuf = stt["i"] % nb
            stt["i"] += 1
            ctr = sems[wbuf]
            W("gpsimd", tT, stt["wrel"][wbuf])
            off = 0
            for src_ap, cols in srcs:
                I("gpsimd", "dma_start",
                  out=tgt[:, 2048 * wbuf + off:2048 * wbuf + off + cols],
                  in_=src_ap, inc=ctr)
                off += cols
            return wbuf, ctr, ctr.n

        def release_unit(wbuf, dst=None):
            (st8 if dst is wb8 else stb)["wrel"][wbuf] = tT.n

        I("sync", "dma_start",
          out=x_sb[:, :].rearrange("p (c t) -> p c t", c=16),
          in_=xin[:, :].rearrange("(c p) t -> p c t", p=128), inc=dX)
        I("gpsimd", "dma_start", out=mask_sb[:, :], in_=mask_in[:, :], inc=dX)
        I("gpsimd", "dma_start", out=iden[:, :], in_=iden_in[:, :], inc=dX)
        I("sync", "dma_start", out=cst[:, 24:48], in_=bq_cols[:, :], inc=dX)
        I("sync", "dma_start", out=cst[:, 48:64], in_=outb_cols[:, :], inc=dX)
        I("sync", "dma_start", out=cst[:, 16:18], in_=subln_cols[:, :], inc=dX)
        I("sync", "dma_start", out=rows[0:1, R_T0:R_T0 + 512], in_=lam_row[:, :], inc=dX)
        I("sync", "dma_start", out=rows[0:1, R_MEAN:R_MEAN + 1024], in_=bqv_row[:, :], inc=dX)
        ld0 = dX.n

        W("vector", dX, ld0)
        I("vector", "memset", onesr[:, :], 1.0)
        I("vector", "memset", onesrf[:, :], 1.0)
        I("vector", "memset", onesc[:, :], 1.0)
        I("vector", "tensor_copy", out=rowsb[0:1, 0:1024], in_=rows[0:1, R_MEAN:R_MEAN + 1024])
        I("vector", "tensor_scalar_mul", cst[:, 16:18], cst[:, 16:18], 1.0 - LAMBDA_INIT)
        I("vector", "tensor_mul", rows[0:1, R_T1:R_T1 + 128],
          rows[0:1, R_T0:R_T0 + 128], rows[0:1, R_T0 + 128:R_T0 + 256])
        I("vector", "tensor_mul", rows[0:1, R_T1 + 128:R_T1 + 256],
          rows[0:1, R_T0 + 256:R_T0 + 384], rows[0:1, R_T0 + 384:R_T0 + 512])
        I("vector", "reduce_sum", rows[0:1, R_LAM + 1:R_LAM + 2],
          rows[0:1, R_T1:R_T1 + 128], axis=AX)
        I("vector", "reduce_sum", rows[0:1, R_LAM + 2:R_LAM + 3],
          rows[0:1, R_T1 + 128:R_T1 + 256], axis=AX, inc=tV)
        W("scalar", tV)
        I("scalar", "activation", rows[0:1, R_LAM + 3:R_LAM + 5],
          rows[0:1, R_LAM + 1:R_LAM + 3], AF.Exp, inc=tA)
        W("vector", tA)
        I("vector", "tensor_scalar", out=rows[0:1, R_LAM:R_LAM + 1],
          in0=rows[0:1, R_LAM + 3:R_LAM + 4],
          scalar1=rows[0:1, R_LAM + 4:R_LAM + 5], scalar2=LAMBDA_INIT,
          op0=ALU.subtract, op1=ALU.add, inc=tV)
        barrier()

        def layernorm(to_fp8=False):
            mm_done = {}
            for ic in range(16):
                W("vector", tT, mm_done.get(ic - 1, 0))
                I("vector", "tensor_copy", out=tmp_a[:, :],
                  in_=x_sb[:, 1024 * ic:1024 * (ic + 1)], inc=tV)
                cast_done = tV.n
                W("tensor", tV, cast_done)
                I("tensor", "matmul", pE[0:1, :], onesc[:, 0:1], tmp_a[:, 0:512],
                  start=(ic == 0), stop=(ic == 15))
                I("tensor", "matmul", pF[0:1, :], onesc[:, 0:1], tmp_a[:, 512:1024],
                  start=(ic == 0), stop=(ic == 15), inc=tT)
                sum_done = tT.n
                W("scalar", tT, sum_done)
                I("scalar", "activation", tmp_a[:, :], tmp_a[:, :], AF.Square, inc=tA)
                W("tensor", tA)
                I("tensor", "matmul", pG[0:1, :], onesc[:, 0:1], tmp_a[:, 0:512],
                  start=(ic == 0), stop=(ic == 15))
                I("tensor", "matmul", pH[0:1, :], onesc[:, 0:1], tmp_a[:, 512:1024],
                  start=(ic == 0), stop=(ic == 15), inc=tT)
                mm_done[ic] = tT.n
            stats_done = tT.n
            W("vector", tT, stats_done)
            I("vector", "tensor_scalar_mul", rows[0:1, R_MEAN:R_MEAN + 512],
              pE[0:1, :], 1.0 / DM)
            I("vector", "tensor_scalar_mul", rows[0:1, R_MEAN + 512:R_MEAN + 1024],
              pF[0:1, :], 1.0 / DM)
            I("vector", "tensor_scalar_mul", rows[0:1, R_T0:R_T0 + 512],
              pG[0:1, :], 1.0 / DM)
            I("vector", "tensor_scalar_mul", rows[0:1, R_T0 + 512:R_T0 + 1024],
              pH[0:1, :], 1.0 / DM)
            for k in range(4):
                a, b = R_MEAN + 256 * k, R_T0 + 256 * k
                I("vector", "tensor_mul", rows[0:1, R_T1:R_T1 + 256],
                  rows[0:1, a:a + 256], rows[0:1, a:a + 256])
                I("vector", "tensor_sub", rows[0:1, b:b + 256],
                  rows[0:1, b:b + 256], rows[0:1, R_T1:R_T1 + 256])
            I("vector", "tensor_scalar_add", rows[0:1, R_T0:R_T0 + 1024],
              rows[0:1, R_T0:R_T0 + 1024], EPS)
            I("vector", "reciprocal", out=rows[0:1, R_T0:R_T0 + 1024],
              in_=rows[0:1, R_T0:R_T0 + 1024], inc=tV)
            W("scalar", tV)
            I("scalar", "activation", rows[0:1, R_T0:R_T0 + 1024],
              rows[0:1, R_T0:R_T0 + 1024], AF.Sqrt, inc=tA)
            for h in range(2):
                W("tensor", tA)
                W("tensor", tV)
                I("tensor", "matmul", pE[:, :], onesrf[0:1, :],
                  rows[0:1, R_MEAN + 512 * h:R_MEAN + 512 * (h + 1)], start=True, stop=True)
                I("tensor", "matmul", pF[:, :], onesrf[0:1, :],
                  rows[0:1, R_T0 + 512 * h:R_T0 + 512 * (h + 1)], start=True, stop=True,
                  inc=tT)
                ydst = y8 if to_fp8 else y_sb
                W("vector", tT)
                for ic in range(16):
                    I("vector", "tensor_sub", tmpf[:, 0:512],
                      x_sb[:, 1024 * ic + 512 * h:1024 * ic + 512 * (h + 1)], pE[:, :])
                    I("vector", "tensor_mul",
                      ydst[:, 1024 * ic + 512 * h:1024 * ic + 512 * (h + 1)],
                      tmpf[:, 0:512], pF[:, :],
                      inc=(tV if ic == 15 else None))

        def ffn():
            gel, mul, res = {}, {}, {}
            for half in range(2):
                for hh in range(32):
                    hc = 32 * half + hh
                    units = {}
                    for m in range(2):
                        wbuf, ctr, rdy = next_unit(
                            [(wg_l[m, hc, :, 0:1024], 1024),
                             (wg_l[m, hc, :, 1024:2048], 1024)], dst=wb8)
                        units[m] = (wbuf, rdy)
                        W("tensor", ctr, rdy)
                        for tb in range(2):
                            pp = (pA, pC)[tb] if m == 0 else (pB, pD)[tb]
                            if m == 0:
                                W("tensor", tA, gel.get((hc - 1, tb), 0))
                            else:
                                W("tensor", tV, mul.get((hc - 1, tb), 0))
                            if half == 1 and hc == 32:
                                W("tensor", tV)
                            for icp in range(8):
                                I("tensor", "matmul", pp[:, :],
                                  wb8[:, 2048 * wbuf + 256 * icp:2048 * wbuf + 256 * (icp + 1)]
                                  .rearrange("p (i o) -> p i o", i=2),
                                  y8_4[:, 2 * icp:2 * icp + 2, tb, :],
                                  start=(icp == 0), stop=(icp == 7), perf_mode=DRM,
                                  inc=(tT if icp == 7 else None))
                            if m == 0:
                                gel[("mmg", hc, tb)] = tT.n
                            else:
                                mul[("mml", hc, tb)] = tT.n
                        release_unit(wbuf, wb8)
                    for tb in range(2):
                        pg = (pA, pC)[tb]
                        pl = (pB, pD)[tb]
                        W("scalar", tT, gel[("mmg", hc, tb)])
                        W("scalar", tV, mul.get((hc - 1, tb), 0))
                        I("scalar", "activation", tmp_a[:, 512 * tb:512 * (tb + 1)],
                          pg[:, :], AF.Gelu_apprx_tanh, inc=tA)
                        gel[(hc, tb)] = tA.n
                        W("vector", tT, mul[("mml", hc, tb)])
                        W("vector", tA, gel[(hc, tb)])
                        I("vector", "tensor_mul",
                          a8[:, 1024 * hh + 512 * tb:1024 * hh + 512 * (tb + 1)],
                          tmp_a[:, 512 * tb:512 * (tb + 1)], pl[:, :], inc=tV)
                        mul[(hc, tb)] = tV.n
                for oc in range(16):
                    if oc == 0:
                        W("tensor", tV)
                    for ku in range(2):
                        wbuf, ctr, rdy = next_unit(
                            [(wl_l[oc, :, 4096 * half + 2048 * ku + 1024 * k:
                                   4096 * half + 2048 * ku + 1024 * (k + 1)], 1024)
                             for k in range(2)], dst=wb8)
                        W("tensor", ctr, rdy)
                        for tb in range(2):
                            pd = (pA, pC)[tb] if oc % 2 == 0 else (pB, pD)[tb]
                            if ku == 0:
                                W("tensor", tV, res.get((oc - 2, tb, half), 0))
                            for hhp in range(8):
                                I("tensor", "matmul", pd[:, :],
                                  wb8[:, 2048 * wbuf + 256 * hhp:2048 * wbuf + 256 * (hhp + 1)]
                                  .rearrange("p (i o) -> p i o", i=2),
                                  a8_4[:, 16 * ku + 2 * hhp:16 * ku + 2 * hhp + 2, tb, :],
                                  start=(ku == 0 and hhp == 0), stop=(ku == 1 and hhp == 7),
                                  perf_mode=DRM,
                                  inc=(tT if hhp == 7 else None))
                            if ku == 1:
                                W("vector", tT)
                                I("vector", "scalar_tensor_tensor",
                                  out=x_sb[:, 1024 * oc + 512 * tb:1024 * oc + 512 * (tb + 1)],
                                  in0=pd[:, :], scalar=0.5,
                                  in1=x_sb[:, 1024 * oc + 512 * tb:1024 * oc + 512 * (tb + 1)],
                                  op0=ALU.mult, op1=ALU.add, inc=tV)
                                res[(oc, tb, half)] = tV.n
                        release_unit(wbuf, wb8)
            return res

        def dump(name, src):
            if DEBUG:
                I("sync", "dma_start", out=dbg[name][:, :], in_=src, inc=dX)

        if DEBUG:
            I("sync", "dma_start", out=dbg["d_rows0"][:, :], in_=rows[:, :], inc=dX)

        layernorm(to_fp8=True)
        barrier()
        ffn()
        barrier()
        dump("d_x1", x_sb[:, :])
        layernorm()
        barrier()
        dump("d_y2", y_sb[:, :])

        qkev = {}
        for oc in range(24):
            wbuf, ctr, rdy = next_unit([(wqkv_l[oc, :, 0:1024], 1024),
                                        (wqkv_l[oc, :, 1024:2048], 1024)])
            W("tensor", ctr, rdy)
            for tb in range(2):
                pq = (pA, pC)[tb] if oc % 2 == 0 else (pB, pD)[tb]
                W("tensor", tV, qkev.get((oc - 2, tb), 0))
                for ic in range(16):
                    I("tensor", "matmul", pq[:, :],
                      wb_sb[:, 2048 * wbuf + 128 * ic:2048 * wbuf + 128 * (ic + 1)],
                      y_sb[:, 1024 * ic + 512 * tb:1024 * ic + 512 * (tb + 1)],
                      start=(ic == 0), stop=(ic == 15),
                      inc=(tT if ic == 15 else None))
                W("vector", tT)
                dst = (A_Q + 1024 * oc if oc < 16 else A_KM + 1024 * (oc - 16)) + 512 * tb
                I("vector", "tensor_scalar_add", aa[:, dst:dst + 512], pq[:, :],
                  cst[:, 24 + oc:25 + oc], inc=tV)
                qkev[(oc, tb)] = tV.n
            release_unit(wbuf)

        vtev = {}
        W("tensor", tV)
        PVB = (pA, pB, pC, pD, pE, pF, pG, pH)
        for jgp in range(2):
            for ic in range(16):
                wbuf, ctr, rdy = next_unit([(wv_r[ic, :, :], 1024)])
                W("tensor", ctr, rdy)
                for jgl in range(2):
                    jg = 2 * jgp + jgl
                    for jj in range(2):
                        j = 2 * jg + jj
                        for vh in range(2):
                            pv = PVB[4 * jgl + 2 * jj + vh]
                            if ic == 0:
                                W("tensor", tA, vtev.get((jg - 2, jj, vh), 0))
                                I("tensor", "matmul", pv[:, :], onesr[0:1, 0:128],
                                  rowsb[0:1, 512 * vh:512 * (vh + 1)], start=True, stop=False)
                            I("tensor", "matmul", pv[:, :],
                              y_sb[:, 1024 * ic + 128 * j:1024 * ic + 128 * (j + 1)],
                              wb_sb[:, 2048 * wbuf + 512 * vh:2048 * wbuf + 512 * (vh + 1)],
                              start=False, stop=(ic == 15),
                              inc=(tT if (ic == 15 or
                                          (jgl == 1 and jj == 1 and vh == 1)) else None))
                            if ic == 15:
                                W("scalar", tT)
                                dst = A_KM + 8192 + 1024 * j + 512 * vh
                                I("scalar", "activation", aa[:, dst:dst + 512], pv[:, :],
                                  AF.Copy, inc=tA)
                                vtev[(jg, jj, vh)] = tA.n
                release_unit(wbuf)

        def norms_ok(base_off, n_chunks, dst_off):
            nred = {}
            for c in range(n_chunks):
                W("scalar", tV, nred.get(c - 1, 0))
                if c == 0:
                    W("scalar", tV)
                src = base_off + 1024 * c
                I("scalar", "activation", tmp_a[:, 0:1024], aa[:, src:src + 1024],
                  AF.Square, inc=tA)
                W("tensor", tA)
                W("tensor", tV, nred.get(c - 1, 0))
                pe = (pE, pF)[c % 2]
                I("tensor", "matmul", pe[0:1, :], onesc[:, 0:1], tmp_a[:, 0:512],
                  start=True, stop=True)
                I("tensor", "matmul", pe[32:33, :], onesc[:, 0:1], tmp_a[:, 512:1024],
                  start=True, stop=True, inc=tT)
                W("vector", tT)
                I("vector", "reduce_max", rows[0:1, R_T1:R_T1 + 1], pe[0:1, :], axis=AX)
                I("vector", "reduce_max", rows[0:1, R_T1 + 1:R_T1 + 2], pe[32:33, :], axis=AX)
                I("vector", "tensor_max", rows[0:1, dst_off + c:dst_off + c + 1],
                  rows[0:1, R_T1:R_T1 + 1], rows[0:1, R_T1 + 1:R_T1 + 2], inc=tV)
                nred[c] = tV.n

        norms_ok(A_Q, 16, R_QN)
        norms_ok(A_KM, 8, R_KN)

        W("sync", tA)
        W("sync", tV)
        I("sync", "dma_start",
          out=ag_in[:, :].rearrange("(u p) t -> p u t", p=128),
          in_=aa[:, A_KM:A_KM + 16384].rearrange("p (u t) -> p u t", u=16), inc=dX)
        I("sync", "dma_start", out=nrm_in[:, :], in_=rows[0:1, R_KN:R_KN + 64], inc=dX)
        W("gpsimd", dX)
        I("gpsimd", "collective_compute", "AllGather", ALU.bypass,
          replica_groups=[[0, 1, 2, 3], [4, 5, 6, 7]],
          ins=[ag_in[:, :]], outs=[ag_out[:, :]], inc=tC)
        I("gpsimd", "collective_compute", "AllGather", ALU.bypass,
          replica_groups=[[0, 1, 2, 3], [4, 5, 6, 7]],
          ins=[nrm_in[:, :]], outs=[nrm_out[:, :]], inc=tC)
        W("sync", tC)
        for r in range(4):
            I("sync", "dma_start", out=rows[0:1, R_T1 + 64 * r:R_T1 + 64 * (r + 1)],
              in_=nrm_out[r:r + 1, :], inc=dX)
        W("vector", dX)
        I("vector", "tensor_max", rows[0:1, R_T1:R_T1 + 64],
          rows[0:1, R_T1:R_T1 + 64], rows[0:1, R_T1 + 64:R_T1 + 128])
        I("vector", "tensor_max", rows[0:1, R_T1 + 128:R_T1 + 192],
          rows[0:1, R_T1 + 128:R_T1 + 192], rows[0:1, R_T1 + 192:R_T1 + 256])
        I("vector", "tensor_max", rows[0:1, R_KA:R_KA + 64],
          rows[0:1, R_T1:R_T1 + 64], rows[0:1, R_T1 + 128:R_T1 + 192])
        for h in range(NH):
            I("vector", "tensor_copy", out=rows[0:1, R_C + 16 + 2 * h:R_C + 18 + 2 * h],
              in_=rows[0:1, R_KA + 2 * (h // 2):R_KA + 2 * (h // 2) + 2])
        I("vector", "tensor_add", rows[0:1, R_C:R_C + 16],
          rows[0:1, R_QN:R_QN + 16], rows[0:1, R_C + 16:R_C + 32])
        I("vector", "tensor_scalar_mul", rows[0:1, R_C:R_C + 16],
          rows[0:1, R_C:R_C + 16], -0.5 * SCALE, inc=tV)
        W("tensor", tV)
        I("tensor", "matmul", pH[:, 0:16], onesrf[0:1, :], rows[0:1, R_C:R_C + 16],
          start=True, stop=True, inc=tT)
        W("scalar", tT)
        I("scalar", "copy", out=cst[:, 0:16], in_=pH[:, 0:16], inc=tA)
        barrier()
        dump("d_aa", aa[:, :])
        dump("d_cst", cst[:, :])
        if DEBUG:
            I("sync", "dma_start", out=dbg["d_rows"][:, :], in_=rows[:, :], inc=dX)

        kv_rel = [0, 0, 0, 0]
        pt_rel = [0, 0, 0, 0]

        def pv_group(tp, c0p, nt, final):
            bufk = tp % 4
            bufp = tp % 4
            for m in range(2):
                for vh in range(2):
                    po = (pA, pB, pC, pD)[2 * m + vh]
                    I("tensor", "matmul", po[:, c0p:512],
                      kv_sb[:, 512 * bufk + 256 + 128 * vh:512 * bufk + 256 + 128 * (vh + 1)],
                      pt_sb[:, 1024 * bufp + 512 * m + c0p:1024 * bufp + 512 * m + 512],
                      start=(tp == 0), stop=(tp == nt - 1))
                pg_s = (pG, pH)[m]
                I("tensor", "matmul", pg_s[0:1, c0p:512], onesc[:, 0:1],
                  pt_sb[:, 1024 * bufp + 512 * m + c0p:1024 * bufp + 512 * m + 512],
                  start=(tp == 0), stop=(tp == nt - 1),
                  inc=(tT if m == 1 else None))
            kv_rel[bufk] = tT.n
            pt_rel[bufp] = tT.n

        def attn_head(gdh, h, qh):
            nt = 16 if qh == 0 else 32
            exp_done = {}
            prev = None
            for t in range(nt):
                jmin = max(t // 4 - 4 * qh, 0)
                c0 = 128 * jmin
                buf = t % 4
                ptb = t % 4
                rt, jt = t % 4, t // 4
                W("sync", tT, kv_rel[buf])
                rbase = 2048 * rt
                I("sync", "dma_start", out=kv_sb[:, 512 * buf:512 * buf + 128],
                  in_=ag_out[rbase + 256 * gdh:rbase + 256 * gdh + 128,
                             128 * jt:128 * (jt + 1)], inc=kvS[buf])
                I("sync", "dma_start", out=kv_sb[:, 512 * buf + 128:512 * buf + 256],
                  in_=ag_out[rbase + 256 * gdh + 128:rbase + 256 * gdh + 256,
                             128 * jt:128 * (jt + 1)], inc=kvS[buf])
                W("gpsimd", tT, kv_rel[buf])
                I("gpsimd", "dma_start", out=kv_sb[:, 512 * buf + 256:512 * buf + 512],
                  in_=ag_out[rbase + 1024 + 128 * jt:rbase + 1024 + 128 * (jt + 1),
                             256 * gdh:256 * (gdh + 1)], inc=kvS[buf])

                W("tensor", kvS[buf])
                masked = (qh == 0) or (t >= 16)
                for m in range(2):
                    ps = (pE, pF)[m]
                    qoff = A_Q + 1024 * (2 * h + m) + 512 * qh
                    if t >= 1:
                        W("tensor", tA, exp_done[(t - 1, m)])
                    I("tensor", "matmul", ps[:, c0:512],
                      kv_sb[:, 512 * buf + 128 * m:512 * buf + 128 * (m + 1)],
                      aa[:, qoff + c0:qoff + 512],
                      start=True, stop=(not masked),
                      inc=(tT if (m == 1 and not masked) else None))
                if masked:
                    for m in range(2):
                        ps = (pE, pF)[m]
                        I("tensor", "matmul", ps[:, c0:c0 + 128], iden[:, :],
                          mask_sb[:, 128 * rt:128 * (rt + 1)],
                          start=False, stop=True,
                          inc=(tT if m == 1 else None))
                scores_done = tT.n

                if prev is not None:
                    pv_group(prev[0], prev[1], nt, False)

                W("scalar", tT, scores_done)
                W("scalar", tT, pt_rel[ptb])
                for m in range(2):
                    ps = (pE, pF)[m]
                    I("scalar", "activation",
                      pt_sb[:, 1024 * ptb + 512 * m + c0:1024 * ptb + 512 * m + 512],
                      ps[:, c0:512], AF.Exp,
                      bias=cst[:, 2 * h + m:2 * h + m + 1], scale=SCALE,
                      inc=tA)
                    exp_done[(t, m)] = tA.n
                prev = (t, c0)

            W("tensor", tA, exp_done[(nt - 1, 1)])
            pv_group(prev[0], prev[1], nt, True)
            pv_done = tT.n

            W("vector", tT, pv_done)
            I("vector", "reciprocal", out=rows[0:1, R_T0:R_T0 + 512], in_=pG[0:1, 0:512])
            I("vector", "reciprocal", out=rows[0:1, R_T0 + 512:R_T0 + 1024],
              in_=pH[0:1, 0:512])
            I("vector", "tensor_scalar", out=rows[0:1, R_T0 + 512:R_T0 + 1024],
              in0=rows[0:1, R_T0 + 512:R_T0 + 1024],
              scalar1=rows[0:1, R_LAM:R_LAM + 1], scalar2=None, op0=ALU.mult, inc=tV)
            rdone = tV.n
            W("tensor", tV, rdone)
            I("tensor", "matmul", pG[:, 0:512], onesrf[0:1, :],
              rows[0:1, R_T0:R_T0 + 512], start=True, stop=True, inc=tT)
            W("scalar", tT)
            I("scalar", "copy", out=tmpf[:, :], in_=pG[:, :], inc=tA)
            W("vector", tA)
            for vh in range(2):
                dst = A_O + 1024 * (2 * h + vh) + 512 * qh
                I("vector", "tensor_mul", aa[:, dst:dst + 512],
                  (pA, pB)[vh][:, :], tmpf[:, :], inc=(tV if vh == 1 else None))
            prod1_done = tV.n
            W("tensor", tV, prod1_done)
            I("tensor", "matmul", pH[:, 0:512], onesrf[0:1, :],
              rows[0:1, R_T0 + 512:R_T0 + 1024], start=True, stop=True, inc=tT)
            W("scalar", tT)
            I("scalar", "copy", out=tmpf[:, :], in_=pH[:, :], inc=tA)
            W("vector", tA)
            for vh in range(2):
                pm = (pC, pD)[vh]
                dst = A_O + 1024 * (2 * h + vh) + 512 * qh
                I("vector", "tensor_mul", pm[:, :], pm[:, :], tmpf[:, :])
                I("vector", "tensor_sub", aa[:, dst:dst + 512],
                  aa[:, dst:dst + 512], pm[:, :], inc=(tV if vh == 1 else None))
            comb_done = tV.n
            W("scalar", tV, comb_done)
            for vh in range(2):
                dst = A_O + 1024 * (2 * h + vh) + 512 * qh
                I("scalar", "activation", tmp_a[:, 512 * vh:512 * (vh + 1)],
                  aa[:, dst:dst + 512], AF.Square, inc=(tA if vh == 1 else None))
            W("tensor", tA)
            for vh in range(2):
                I("tensor", "matmul", pG[0:1, 0:512], onesc[:, 0:1],
                  tmp_a[:, 512 * vh:512 * (vh + 1)], start=(vh == 0), stop=(vh == 1),
                  inc=(tT if vh == 1 else None))
            W("vector", tT)
            I("vector", "tensor_scalar", out=rows[0:1, R_T0:R_T0 + 512],
              in0=pG[0:1, 0:512], scalar1=1.0 / VD, scalar2=EPS,
              op0=ALU.mult, op1=ALU.add)
            I("vector", "reciprocal", out=rows[0:1, R_T0:R_T0 + 512],
              in_=rows[0:1, R_T0:R_T0 + 512], inc=tV)
            W("scalar", tV)
            I("scalar", "activation", rows[0:1, R_T0 + 512:R_T0 + 1024],
              rows[0:1, R_T0:R_T0 + 512], AF.Sqrt, inc=tA)
            W("tensor", tA)
            I("tensor", "matmul", pH[:, 0:512], onesrf[0:1, :],
              rows[0:1, R_T0 + 512:R_T0 + 1024], start=True, stop=True, inc=tT)
            W("scalar", tT)
            I("scalar", "copy", out=tmpf[:, :], in_=pH[:, :], inc=tA)
            W("vector", tA)
            for vh in range(2):
                dst = A_O + 1024 * (2 * h + vh) + 512 * qh
                I("vector", "scalar_tensor_tensor",
                  out=aa[:, dst:dst + 512], in0=aa[:, dst:dst + 512],
                  scalar=cst[:, 16 + vh:17 + vh], in1=tmpf[:, :],
                  op0=ALU.mult, op1=ALU.mult, inc=tV)

        for gdh in range(NKV):
            for hh2 in range(2):
                h = 2 * gdh + hh2
                for qh in range(2):
                    attn_head(gdh, h, qh)
                    barrier(("tensor", "vector", "scalar"))

        dump("d_o", aa[:, A_O:A_O + 16384])

        ores = {}
        for oc in range(16):
            wbuf, ctr, rdy = next_unit([(outw_l[oc, :, 0:1024], 1024),
                                        (outw_l[oc, :, 1024:2048], 1024)])
            W("tensor", ctr, rdy)
            for tb in range(2):
                po = (pA, pC)[tb] if oc % 2 == 0 else (pB, pD)[tb]
                W("tensor", tV, ores.get((oc - 2, tb), 0))
                for ic in range(16):
                    I("tensor", "matmul", po[:, :],
                      wb_sb[:, 2048 * wbuf + 128 * ic:2048 * wbuf + 128 * (ic + 1)],
                      aa[:, A_O + 1024 * ic + 512 * tb:A_O + 1024 * ic + 512 * (tb + 1)],
                      start=(ic == 0), stop=(ic == 15),
                      inc=(tT if ic == 15 else None))
                W("vector", tT)
                I("vector", "scalar_tensor_tensor",
                  out=x_sb[:, 1024 * oc + 512 * tb:1024 * oc + 512 * (tb + 1)],
                  in0=po[:, :], scalar=cst[:, 48 + oc:49 + oc],
                  in1=x_sb[:, 1024 * oc + 512 * tb:1024 * oc + 512 * (tb + 1)],
                  op0=ALU.add, op1=ALU.add, inc=tV)
                ores[(oc, tb)] = tV.n
            release_unit(wbuf)
        barrier()
        dump("d_x2", x_sb[:, :])

        layernorm(to_fp8=True)
        barrier()
        res3 = ffn()

        for oc in range(16):
            W("sync", tV, res3[(oc, 1, 1)])
            I("sync", "dma_start",
              out=xout[128 * oc:128 * (oc + 1), :],
              in_=x_sb[:, 1024 * oc:1024 * (oc + 1)], inc=dX)
        W("sync", dX)

        block = ctx.enter_context(nc.Block())

        @block.sync
        def _(e):
            for f in prog["sync"]:
                f(e)

        @block.gpsimd
        def _(e):
            for f in prog["gpsimd"]:
                f(e)

        @block.vector
        def _(e):
            for f in prog["vector"]:
                f(e)

        @block.scalar
        def _(e):
            for f in prog["scalar"]:
                f(e)

        @block.tensor
        def _(e):
            for f in prog["tensor"]:
                f(e)

    return nc


_NC_CACHE = None


def kernel(L0_attn_hidden_states, L0_attn_Wqkv_w, L0_attn_Wqkv_b,
           L0_attn_lambda_q1, L0_attn_lambda_k1, L0_attn_lambda_q2,
           L0_attn_lambda_k2, L0_attn_subln_w, L0_attn_out_w, L0_attn_out_b,
           L0_ffn_w_gating, L0_ffn_w_linear):
    global _NC_CACHE
    x0 = np.asarray(L0_attn_hidden_states, dtype=np.float32)
    Wqkv = np.asarray(L0_attn_Wqkv_w, dtype=np.float32)
    bqkv = np.asarray(L0_attn_Wqkv_b, dtype=np.float32)
    outw = np.asarray(L0_attn_out_w, dtype=np.float32)
    outb = np.asarray(L0_attn_out_b, dtype=np.float32)
    wg = np.asarray(L0_ffn_w_gating, dtype=np.float32)
    wl = np.asarray(L0_ffn_w_linear, dtype=np.float32)
    subln = np.asarray(L0_attn_subln_w, dtype=np.float32)

    wqkv_l = np.ascontiguousarray(
        Wqkv[:3072].reshape(24, 128, 16, 128).transpose(0, 3, 2, 1).reshape(24, 128, 2048))
    wv_r = np.ascontiguousarray(Wqkv[3072:].reshape(1024, 16, 128).transpose(1, 2, 0))
    outw_l = np.ascontiguousarray(
        outw.reshape(16, 128, 16, 128).transpose(0, 3, 2, 1).reshape(16, 128, 2048))
    wg_l = np.ascontiguousarray(
        wg.reshape(2, 16, 128, 64, 128).transpose(0, 3, 2, 1, 4).reshape(2, 64, 128, 2048))
    wl_l = np.ascontiguousarray(
        wl.reshape(64, 128, 16, 128).transpose(2, 1, 0, 3).reshape(16, 128, 8192))
    bq_cols = np.ascontiguousarray(bqkv[:3072].reshape(24, 128).T)
    bqv_row = np.ascontiguousarray(bqkv[3072:].reshape(1, 1024))
    outb_cols = np.ascontiguousarray(outb.reshape(16, 128).T)
    subln_cols = np.ascontiguousarray(subln.reshape(2, 128).T)
    lam_row = np.ascontiguousarray(np.concatenate(
        [np.asarray(L0_attn_lambda_q1, dtype=np.float32),
         np.asarray(L0_attn_lambda_k1, dtype=np.float32),
         np.asarray(L0_attn_lambda_q2, dtype=np.float32),
         np.asarray(L0_attn_lambda_k2, dtype=np.float32)]).reshape(1, 512))

    shared = dict(wqkv_l=wqkv_l, wv_r=wv_r, outw_l=outw_l, wg_l=wg_l, wl_l=wl_l,
                  bq_cols=bq_cols, bqv_row=bqv_row, outb_cols=outb_cols,
                  subln_cols=subln_cols, lam_row=lam_row)

    tok_idx = {}
    in_maps = []
    kl = np.arange(128)[:, None]
    ql = np.arange(128)[None, :]
    for c in range(8):
        b, r = c // 4, c % 4
        idx = np.concatenate([np.arange(128 * (4 * j + r), 128 * (4 * j + r) + 128)
                              for j in range(8)])
        tok_idx[c] = (b, idx)
        mask = np.zeros((128, 512), dtype=np.float32)
        for rr in range(4):
            mask[:, 128 * rr:128 * (rr + 1)] = np.where(
                128 * rr + kl > 128 * r + ql, NEG, 0.0)
        m = dict(shared)
        m["xin"] = np.ascontiguousarray(x0[b, idx, :].T)
        m["mask_in"] = mask
        m["iden_in"] = np.eye(128, dtype=np.float32)
        in_maps.append(m)

    if _NC_CACHE is None:
        _NC_CACHE = build_program()
    res = run_bass_kernel_spmd(_NC_CACHE, in_maps, core_ids=list(range(8)))
    globals()["_LAST_RES"] = res

    out = np.empty((B, S, DM), dtype=np.float32)
    for c in range(8):
        b, idx = tok_idx[c]
        out[b, idx, :] = res.results[c]["xout"].T
    return out


if __name__ == "__main__":
    import reference
    inputs = reference.setup_inputs(0)
    inputs = {k: np.asarray(v) for k, v in inputs.items()}
    got = kernel(**inputs)
    exp = np.asarray(reference.reference(**inputs))
    err = np.mean((got - exp) ** 2) / np.mean(exp ** 2)
    print("residual variance ratio:", err)
    print("max abs diff:", np.abs(got - exp).max())


# version 16 (final)
# speedup vs baseline: 1.0416x; 1.0085x over previous
import math
from contextlib import ExitStack

import numpy as np

import concourse.bass as bass
import concourse.mybir as mybir
from concourse.bass_utils import run_bass_kernel_spmd

F32 = mybir.dt.float32
BF16 = mybir.dt.bfloat16
F8 = mybir.dt.float8e4
AF = mybir.ActivationFunctionType
ALU = mybir.AluOpType
AX = mybir.AxisListType.X

B = 2
S = 4096
DM = 2048
HID = 8192
NH = 8
NKV = 4
D = 128
VD = 256
TPC = 1024
LAMBDA_INIT = 0.8 - 0.6 * math.exp(-0.3 * 1)
EPS = 1e-5
SCALE = D ** -0.5
NEG = -30000.0

ENGINES = ("sync", "gpsimd", "vector", "scalar", "tensor")
DEBUG = False
_LAST_RES = None


class Ctr:
    def __init__(self, sem, step):
        self.sem = sem
        self.step = step
        self.n = 0


def build_program():
    nc = bass.Bass("TRN2", target_bir_lowering=False)

    xin = nc.dram_tensor("xin", [DM, TPC], F32, kind="ExternalInput")
    wqkv_l = nc.dram_tensor("wqkv_l", [24, 128, DM], F32, kind="ExternalInput")
    wv_r = nc.dram_tensor("wv_r", [16, 128, 1024], F32, kind="ExternalInput")
    outw_l = nc.dram_tensor("outw_l", [16, 128, DM], F32, kind="ExternalInput")
    wg_l = nc.dram_tensor("wg_l", [2, 64, 128, DM], F32, kind="ExternalInput")
    wl_l = nc.dram_tensor("wl_l", [16, 128, HID], F32, kind="ExternalInput")
    bq_cols = nc.dram_tensor("bq_cols", [128, 24], F32, kind="ExternalInput")
    bqv_row = nc.dram_tensor("bqv_row", [1, 1024], F32, kind="ExternalInput")
    outb_cols = nc.dram_tensor("outb_cols", [128, 16], F32, kind="ExternalInput")
    subln_cols = nc.dram_tensor("subln_cols", [128, 2], F32, kind="ExternalInput")
    lam_row = nc.dram_tensor("lam_row", [1, 512], F32, kind="ExternalInput")
    mask_in = nc.dram_tensor("mask_in", [128, 512], F32, kind="ExternalInput")
    iden_in = nc.dram_tensor("iden_in", [128, 128], F32, kind="ExternalInput")
    xout = nc.dram_tensor("xout", [DM, TPC], F32, kind="ExternalOutput")

    dbg = {}
    if DEBUG:
        dbg["d_y1"] = nc.dram_tensor("d_y1", [128, 16384], BF16, kind="ExternalOutput")
        dbg["d_x1"] = nc.dram_tensor("d_x1", [128, 16384], F32, kind="ExternalOutput")
        dbg["d_y2"] = nc.dram_tensor("d_y2", [128, 16384], BF16, kind="ExternalOutput")
        dbg["d_aa"] = nc.dram_tensor("d_aa", [128, 32768], BF16, kind="ExternalOutput")
        dbg["d_cst"] = nc.dram_tensor("d_cst", [128, 64], F32, kind="ExternalOutput")
        dbg["d_rows"] = nc.dram_tensor("d_rows", [1, 2576], F32, kind="ExternalOutput")
        dbg["d_rows0"] = nc.dram_tensor("d_rows0", [1, 2576], F32, kind="ExternalOutput")
        dbg["d_o"] = nc.dram_tensor("d_o", [128, 16384], BF16, kind="ExternalOutput")
        dbg["d_x2"] = nc.dram_tensor("d_x2", [128, 16384], F32, kind="ExternalOutput")

    ag_in = nc.dram_tensor("ag_in", [2048, 1024], BF16)
    ag_out = nc.dram_tensor("ag_out", [8192, 1024], BF16)
    nrm_in = nc.dram_tensor("nrm_in", [1, 64], F32)
    nrm_out = nc.dram_tensor("nrm_out", [4, 64], F32)

    with ExitStack() as ctx:
        sbt = lambda n, s, d: ctx.enter_context(nc.sbuf_tensor(n, s, d))
        pst = lambda n: ctx.enter_context(nc.psum_tensor(n, [128, 512], F32))
        mks = lambda n, st: Ctr(ctx.enter_context(nc.semaphore(n)), st)

        x_sb = sbt("x_sb", [128, 16 * 1024], F32)
        aa = sbt("aa", [128, 32 * 1024], BF16)
        y_sb = sbt("y_sb", [128, 16 * 1024], BF16)
        wb_sb = sbt("wb_sb", [128, 2 * 2048], BF16)
        wb8 = sbt("wb8", [128, 4 * 2048], F8)
        pt_sb = sbt("pt_sb", [128, 4 * 1024], BF16)
        kv_sb = sbt("kv_sb", [128, 4 * 512], BF16)
        mask_sb = sbt("mask_sb", [128, 512], BF16)
        iden = sbt("iden", [128, 128], BF16)
        tmp_a = sbt("tmp_a", [128, 1024], BF16)
        tmpf = sbt("tmpf", [128, 512], F32)
        cst = sbt("cst", [128, 64], F32)
        rows = sbt("rows", [1, 2576], F32)
        rowsb = sbt("rowsb", [1, 1024], BF16)
        onesr = sbt("onesr", [1, 512], BF16)
        onesrf = sbt("onesrf", [1, 128], F32)
        onesc = sbt("onesc", [128, 1], BF16)

        R_MEAN = 0
        R_T0 = 1024
        R_T1 = 2048
        R_QN = 2304
        R_KN = 2368
        R_KA = 2432
        R_C = 2496
        R_LAM = 2560
        A_Q = 0
        A_KM = 16 * 1024
        A_O = 16 * 1024

        a8 = aa[:, :].bitcast(F8)
        y8 = y_sb[:, :].bitcast(F8)
        y8_4 = y8[:, 0:16384].rearrange("p (ic tb j) -> p ic tb j", ic=16, tb=2, j=512)
        a8_4 = a8[:, 0:32768].rearrange("p (hh tb j) -> p hh tb j", hh=32, tb=2, j=512)
        DRM = mybir.MatmulPerfMode.DoubleRow

        pA, pB, pC, pD = pst("pA"), pst("pB"), pst("pC"), pst("pD")
        pE, pF, pG, pH = pst("pE"), pst("pF"), pst("pG"), pst("pH")

        dX = mks("dX", 16)
        dU8 = [mks("dU8_%d" % i, 16) for i in range(4)]
        dUb = [mks("dUb_%d" % i, 16) for i in range(2)]
        kvS = [mks("kv%d" % i, 16) for i in range(4)]
        tT = mks("tT", 1)
        tV = mks("tV", 1)
        tA = mks("tA", 1)
        tC = mks("tC", 1)

        prog = {e: [] for e in ENGINES}

        def W(eng, c, v=None):
            val = c.n if v is None else v
            if val <= 0:
                return
            prog[eng].append(lambda e, s=c.sem, vv=val: e.wait_ge(s, vv))

        def I(eng, meth, *args, inc=None, **kw):
            if inc is not None:
                inc.n += inc.step
            def f(e, meth=meth, args=args, kw=kw, inc=inc):
                ins = getattr(e, meth)(*args, **kw)
                if inc is not None:
                    ins.then_inc(inc.sem, inc.step)
            prog[eng].append(f)

        def barrier(engs=ENGINES):
            snap = [(c, c.n) for c in
                    (dX, dU8[0], dU8[1], dU8[2], dU8[3], dUb[0], dUb[1],
                     kvS[0], kvS[1], kvS[2], kvS[3], tT, tV, tA, tC)]
            for e in engs:
                for c, v in snap:
                    W(e, c, v)

        st8 = {"wrel": [0, 0, 0, 0], "i": 0}
        stb = {"wrel": [0, 0], "i": 0}

        def next_unit(srcs, dst=None):
            stt, nb, sems, tgt = \
                (st8, 4, dU8, wb8) if dst is wb8 else (stb, 2, dUb, wb_sb)
            wbuf = stt["i"] % nb
            stt["i"] += 1
            ctr = sems[wbuf]
            W("gpsimd", tT, stt["wrel"][wbuf])
            off = 0
            for src_ap, cols in srcs:
                I("gpsimd", "dma_start",
                  out=tgt[:, 2048 * wbuf + off:2048 * wbuf + off + cols],
                  in_=src_ap, inc=ctr)
                off += cols
            return wbuf, ctr, ctr.n

        def release_unit(wbuf, dst=None):
            (st8 if dst is wb8 else stb)["wrel"][wbuf] = tT.n

        I("sync", "dma_start",
          out=x_sb[:, :].rearrange("p (c t) -> p c t", c=16),
          in_=xin[:, :].rearrange("(c p) t -> p c t", p=128), inc=dX)
        I("gpsimd", "dma_start", out=mask_sb[:, :], in_=mask_in[:, :], inc=dX)
        I("gpsimd", "dma_start", out=iden[:, :], in_=iden_in[:, :], inc=dX)
        I("sync", "dma_start", out=cst[:, 24:48], in_=bq_cols[:, :], inc=dX)
        I("sync", "dma_start", out=cst[:, 48:64], in_=outb_cols[:, :], inc=dX)
        I("sync", "dma_start", out=cst[:, 16:18], in_=subln_cols[:, :], inc=dX)
        I("sync", "dma_start", out=rows[0:1, R_T0:R_T0 + 512], in_=lam_row[:, :], inc=dX)
        I("sync", "dma_start", out=rows[0:1, R_MEAN:R_MEAN + 1024], in_=bqv_row[:, :], inc=dX)
        ld0 = dX.n

        W("vector", dX, ld0)
        I("vector", "memset", onesr[:, :], 1.0)
        I("vector", "memset", onesrf[:, :], 1.0)
        I("vector", "memset", onesc[:, :], 1.0)
        I("vector", "tensor_copy", out=rowsb[0:1, 0:1024], in_=rows[0:1, R_MEAN:R_MEAN + 1024])
        I("vector", "tensor_scalar_mul", cst[:, 16:18], cst[:, 16:18], 1.0 - LAMBDA_INIT)
        I("vector", "tensor_mul", rows[0:1, R_T1:R_T1 + 128],
          rows[0:1, R_T0:R_T0 + 128], rows[0:1, R_T0 + 128:R_T0 + 256])
        I("vector", "tensor_mul", rows[0:1, R_T1 + 128:R_T1 + 256],
          rows[0:1, R_T0 + 256:R_T0 + 384], rows[0:1, R_T0 + 384:R_T0 + 512])
        I("vector", "reduce_sum", rows[0:1, R_LAM + 1:R_LAM + 2],
          rows[0:1, R_T1:R_T1 + 128], axis=AX)
        I("vector", "reduce_sum", rows[0:1, R_LAM + 2:R_LAM + 3],
          rows[0:1, R_T1 + 128:R_T1 + 256], axis=AX, inc=tV)
        W("scalar", tV)
        I("scalar", "activation", rows[0:1, R_LAM + 3:R_LAM + 5],
          rows[0:1, R_LAM + 1:R_LAM + 3], AF.Exp, inc=tA)
        W("vector", tA)
        I("vector", "tensor_scalar", out=rows[0:1, R_LAM:R_LAM + 1],
          in0=rows[0:1, R_LAM + 3:R_LAM + 4],
          scalar1=rows[0:1, R_LAM + 4:R_LAM + 5], scalar2=LAMBDA_INIT,
          op0=ALU.subtract, op1=ALU.add, inc=tV)
        barrier()

        def layernorm(to_fp8=False):
            mm_done = {}
            for ic in range(16):
                W("vector", tT, mm_done.get(ic - 1, 0))
                I("vector", "tensor_copy", out=tmp_a[:, :],
                  in_=x_sb[:, 1024 * ic:1024 * (ic + 1)], inc=tV)
                cast_done = tV.n
                W("tensor", tV, cast_done)
                I("tensor", "matmul", pE[0:1, :], onesc[:, 0:1], tmp_a[:, 0:512],
                  start=(ic == 0), stop=(ic == 15))
                I("tensor", "matmul", pF[0:1, :], onesc[:, 0:1], tmp_a[:, 512:1024],
                  start=(ic == 0), stop=(ic == 15), inc=tT)
                sum_done = tT.n
                W("scalar", tT, sum_done)
                I("scalar", "activation", tmp_a[:, :], tmp_a[:, :], AF.Square, inc=tA)
                W("tensor", tA)
                I("tensor", "matmul", pG[0:1, :], onesc[:, 0:1], tmp_a[:, 0:512],
                  start=(ic == 0), stop=(ic == 15))
                I("tensor", "matmul", pH[0:1, :], onesc[:, 0:1], tmp_a[:, 512:1024],
                  start=(ic == 0), stop=(ic == 15), inc=tT)
                mm_done[ic] = tT.n
            stats_done = tT.n
            W("vector", tT, stats_done)
            I("vector", "tensor_scalar_mul", rows[0:1, R_MEAN:R_MEAN + 512],
              pE[0:1, :], 1.0 / DM)
            I("vector", "tensor_scalar_mul", rows[0:1, R_MEAN + 512:R_MEAN + 1024],
              pF[0:1, :], 1.0 / DM)
            I("vector", "tensor_scalar_mul", rows[0:1, R_T0:R_T0 + 512],
              pG[0:1, :], 1.0 / DM)
            I("vector", "tensor_scalar_mul", rows[0:1, R_T0 + 512:R_T0 + 1024],
              pH[0:1, :], 1.0 / DM)
            for k in range(4):
                a, b = R_MEAN + 256 * k, R_T0 + 256 * k
                I("vector", "tensor_mul", rows[0:1, R_T1:R_T1 + 256],
                  rows[0:1, a:a + 256], rows[0:1, a:a + 256])
                I("vector", "tensor_sub", rows[0:1, b:b + 256],
                  rows[0:1, b:b + 256], rows[0:1, R_T1:R_T1 + 256])
            I("vector", "tensor_scalar_add", rows[0:1, R_T0:R_T0 + 1024],
              rows[0:1, R_T0:R_T0 + 1024], EPS)
            I("vector", "reciprocal", out=rows[0:1, R_T0:R_T0 + 1024],
              in_=rows[0:1, R_T0:R_T0 + 1024], inc=tV)
            W("scalar", tV)
            I("scalar", "activation", rows[0:1, R_T0:R_T0 + 1024],
              rows[0:1, R_T0:R_T0 + 1024], AF.Sqrt, inc=tA)
            for h in range(2):
                W("tensor", tA)
                W("tensor", tV)
                I("tensor", "matmul", pE[:, :], onesrf[0:1, :],
                  rows[0:1, R_MEAN + 512 * h:R_MEAN + 512 * (h + 1)], start=True, stop=True)
                I("tensor", "matmul", pF[:, :], onesrf[0:1, :],
                  rows[0:1, R_T0 + 512 * h:R_T0 + 512 * (h + 1)], start=True, stop=True,
                  inc=tT)
                ydst = y8 if to_fp8 else y_sb
                W("vector", tT)
                for ic in range(16):
                    I("vector", "tensor_sub", tmpf[:, 0:512],
                      x_sb[:, 1024 * ic + 512 * h:1024 * ic + 512 * (h + 1)], pE[:, :])
                    I("vector", "tensor_mul",
                      ydst[:, 1024 * ic + 512 * h:1024 * ic + 512 * (h + 1)],
                      tmpf[:, 0:512], pF[:, :],
                      inc=(tV if ic == 15 else None))

        def ffn():
            gel, mul, res = {}, {}, {}
            for half in range(2):
                for hh in range(32):
                    hc = 32 * half + hh
                    units = {}
                    for m in range(2):
                        wbuf, ctr, rdy = next_unit(
                            [(wg_l[m, hc, :, 0:1024], 1024),
                             (wg_l[m, hc, :, 1024:2048], 1024)], dst=wb8)
                        units[m] = (wbuf, rdy)
                        W("tensor", ctr, rdy)
                        for tb in range(2):
                            pp = (pA, pC)[tb] if m == 0 else (pB, pD)[tb]
                            if m == 0:
                                W("tensor", tA, gel.get((hc - 1, tb), 0))
                            else:
                                W("tensor", tV, mul.get((hc - 1, tb), 0))
                            if half == 1 and hc == 32:
                                W("tensor", tV)
                            for icp in range(8):
                                I("tensor", "matmul", pp[:, :],
                                  wb8[:, 2048 * wbuf + 256 * icp:2048 * wbuf + 256 * (icp + 1)]
                                  .rearrange("p (i o) -> p i o", i=2),
                                  y8_4[:, 2 * icp:2 * icp + 2, tb, :],
                                  start=(icp == 0), stop=(icp == 7), perf_mode=DRM,
                                  inc=(tT if icp == 7 else None))
                            if m == 0:
                                gel[("mmg", hc, tb)] = tT.n
                            else:
                                mul[("mml", hc, tb)] = tT.n
                        release_unit(wbuf, wb8)
                    for tb in range(2):
                        pg = (pA, pC)[tb]
                        pl = (pB, pD)[tb]
                        W("scalar", tT, gel[("mmg", hc, tb)])
                        W("scalar", tV, mul.get((hc - 1, tb), 0))
                        I("scalar", "activation", tmp_a[:, 512 * tb:512 * (tb + 1)],
                          pg[:, :], AF.Gelu_apprx_tanh, inc=tA)
                        gel[(hc, tb)] = tA.n
                        W("vector", tT, mul[("mml", hc, tb)])
                        W("vector", tA, gel[(hc, tb)])
                        I("vector", "tensor_mul",
                          a8[:, 1024 * hh + 512 * tb:1024 * hh + 512 * (tb + 1)],
                          tmp_a[:, 512 * tb:512 * (tb + 1)], pl[:, :], inc=tV)
                        mul[(hc, tb)] = tV.n
                for oc in range(16):
                    if oc == 0:
                        W("tensor", tV)
                    for ku in range(2):
                        wbuf, ctr, rdy = next_unit(
                            [(wl_l[oc, :, 4096 * half + 2048 * ku + 1024 * k:
                                   4096 * half + 2048 * ku + 1024 * (k + 1)], 1024)
                             for k in range(2)], dst=wb8)
                        W("tensor", ctr, rdy)
                        for tb in range(2):
                            pd = (pA, pC)[tb] if oc % 2 == 0 else (pB, pD)[tb]
                            if ku == 0:
                                W("tensor", tV, res.get((oc - 2, tb, half), 0))
                            for hhp in range(8):
                                I("tensor", "matmul", pd[:, :],
                                  wb8[:, 2048 * wbuf + 256 * hhp:2048 * wbuf + 256 * (hhp + 1)]
                                  .rearrange("p (i o) -> p i o", i=2),
                                  a8_4[:, 16 * ku + 2 * hhp:16 * ku + 2 * hhp + 2, tb, :],
                                  start=(ku == 0 and hhp == 0), stop=(ku == 1 and hhp == 7),
                                  perf_mode=DRM,
                                  inc=(tT if hhp == 7 else None))
                            if ku == 1:
                                W("vector", tT)
                                I("vector", "scalar_tensor_tensor",
                                  out=x_sb[:, 1024 * oc + 512 * tb:1024 * oc + 512 * (tb + 1)],
                                  in0=pd[:, :], scalar=0.5,
                                  in1=x_sb[:, 1024 * oc + 512 * tb:1024 * oc + 512 * (tb + 1)],
                                  op0=ALU.mult, op1=ALU.add, inc=tV)
                                res[(oc, tb, half)] = tV.n
                        release_unit(wbuf, wb8)
            return res

        def dump(name, src):
            if DEBUG:
                I("sync", "dma_start", out=dbg[name][:, :], in_=src, inc=dX)

        if DEBUG:
            I("sync", "dma_start", out=dbg["d_rows0"][:, :], in_=rows[:, :], inc=dX)

        layernorm(to_fp8=True)
        barrier()
        ffn()
        barrier()
        dump("d_x1", x_sb[:, :])
        layernorm()
        barrier()
        dump("d_y2", y_sb[:, :])

        qkev = {}
        for oc in range(24):
            wbuf, ctr, rdy = next_unit([(wqkv_l[oc, :, 0:1024], 1024),
                                        (wqkv_l[oc, :, 1024:2048], 1024)])
            W("tensor", ctr, rdy)
            for tb in range(2):
                pq = (pA, pC)[tb] if oc % 2 == 0 else (pB, pD)[tb]
                W("tensor", tV, qkev.get((oc - 2, tb), 0))
                for ic in range(16):
                    I("tensor", "matmul", pq[:, :],
                      wb_sb[:, 2048 * wbuf + 128 * ic:2048 * wbuf + 128 * (ic + 1)],
                      y_sb[:, 1024 * ic + 512 * tb:1024 * ic + 512 * (tb + 1)],
                      start=(ic == 0), stop=(ic == 15),
                      inc=(tT if ic == 15 else None))
                W("vector", tT)
                dst = (A_Q + 1024 * oc if oc < 16 else A_KM + 1024 * (oc - 16)) + 512 * tb
                I("vector", "tensor_scalar_add", aa[:, dst:dst + 512], pq[:, :],
                  cst[:, 24 + oc:25 + oc], inc=tV)
                qkev[(oc, tb)] = tV.n
            release_unit(wbuf)

        vtev = {}
        W("tensor", tV)
        PVB = (pA, pB, pC, pD, pE, pF, pG, pH)
        for jgp in range(2):
            for ic in range(16):
                wbuf, ctr, rdy = next_unit([(wv_r[ic, :, :], 1024)])
                W("tensor", ctr, rdy)
                for jgl in range(2):
                    jg = 2 * jgp + jgl
                    for jj in range(2):
                        j = 2 * jg + jj
                        for vh in range(2):
                            pv = PVB[4 * jgl + 2 * jj + vh]
                            if ic == 0:
                                W("tensor", tA, vtev.get((jg - 2, jj, vh), 0))
                                I("tensor", "matmul", pv[:, :], onesr[0:1, 0:128],
                                  rowsb[0:1, 512 * vh:512 * (vh + 1)], start=True, stop=False)
                            I("tensor", "matmul", pv[:, :],
                              y_sb[:, 1024 * ic + 128 * j:1024 * ic + 128 * (j + 1)],
                              wb_sb[:, 2048 * wbuf + 512 * vh:2048 * wbuf + 512 * (vh + 1)],
                              start=False, stop=(ic == 15),
                              inc=(tT if (ic == 15 or
                                          (jgl == 1 and jj == 1 and vh == 1)) else None))
                            if ic == 15:
                                W("scalar", tT)
                                dst = A_KM + 8192 + 1024 * j + 512 * vh
                                I("scalar", "activation", aa[:, dst:dst + 512], pv[:, :],
                                  AF.Copy, inc=tA)
                                vtev[(jg, jj, vh)] = tA.n
                release_unit(wbuf)

        def norms_ok(base_off, n_chunks, dst_off):
            nred = {}
            for c in range(n_chunks):
                W("scalar", tV, nred.get(c - 1, 0))
                if c == 0:
                    W("scalar", tV)
                src = base_off + 1024 * c
                I("scalar", "activation", tmp_a[:, 0:1024], aa[:, src:src + 1024],
                  AF.Square, inc=tA)
                W("tensor", tA)
                W("tensor", tV, nred.get(c - 1, 0))
                pe = (pE, pF)[c % 2]
                I("tensor", "matmul", pe[0:1, :], onesc[:, 0:1], tmp_a[:, 0:512],
                  start=True, stop=True)
                I("tensor", "matmul", pe[32:33, :], onesc[:, 0:1], tmp_a[:, 512:1024],
                  start=True, stop=True, inc=tT)
                W("vector", tT)
                I("vector", "reduce_max", rows[0:1, R_T1:R_T1 + 1], pe[0:1, :], axis=AX)
                I("vector", "reduce_max", rows[0:1, R_T1 + 1:R_T1 + 2], pe[32:33, :], axis=AX)
                I("vector", "tensor_max", rows[0:1, dst_off + c:dst_off + c + 1],
                  rows[0:1, R_T1:R_T1 + 1], rows[0:1, R_T1 + 1:R_T1 + 2], inc=tV)
                nred[c] = tV.n

        norms_ok(A_Q, 16, R_QN)
        norms_ok(A_KM, 8, R_KN)

        W("sync", tA)
        W("sync", tV)
        I("sync", "dma_start",
          out=ag_in[:, :].rearrange("(u p) t -> p u t", p=128),
          in_=aa[:, A_KM:A_KM + 16384].rearrange("p (u t) -> p u t", u=16), inc=dX)
        I("sync", "dma_start", out=nrm_in[:, :], in_=rows[0:1, R_KN:R_KN + 64], inc=dX)
        W("gpsimd", dX)
        I("gpsimd", "collective_compute", "AllGather", ALU.bypass,
          replica_groups=[[0, 1, 2, 3], [4, 5, 6, 7]],
          ins=[ag_in[:, :]], outs=[ag_out[:, :]], inc=tC)
        I("gpsimd", "collective_compute", "AllGather", ALU.bypass,
          replica_groups=[[0, 1, 2, 3], [4, 5, 6, 7]],
          ins=[nrm_in[:, :]], outs=[nrm_out[:, :]], inc=tC)
        W("sync", tC)
        for r in range(4):
            I("sync", "dma_start", out=rows[0:1, R_T1 + 64 * r:R_T1 + 64 * (r + 1)],
              in_=nrm_out[r:r + 1, :], inc=dX)
        W("vector", dX)
        I("vector", "tensor_max", rows[0:1, R_T1:R_T1 + 64],
          rows[0:1, R_T1:R_T1 + 64], rows[0:1, R_T1 + 64:R_T1 + 128])
        I("vector", "tensor_max", rows[0:1, R_T1 + 128:R_T1 + 192],
          rows[0:1, R_T1 + 128:R_T1 + 192], rows[0:1, R_T1 + 192:R_T1 + 256])
        I("vector", "tensor_max", rows[0:1, R_KA:R_KA + 64],
          rows[0:1, R_T1:R_T1 + 64], rows[0:1, R_T1 + 128:R_T1 + 192])
        for h in range(NH):
            I("vector", "tensor_copy", out=rows[0:1, R_C + 16 + 2 * h:R_C + 18 + 2 * h],
              in_=rows[0:1, R_KA + 2 * (h // 2):R_KA + 2 * (h // 2) + 2])
        I("vector", "tensor_add", rows[0:1, R_C:R_C + 16],
          rows[0:1, R_QN:R_QN + 16], rows[0:1, R_C + 16:R_C + 32])
        I("vector", "tensor_scalar_mul", rows[0:1, R_C:R_C + 16],
          rows[0:1, R_C:R_C + 16], -0.5 * SCALE, inc=tV)
        W("tensor", tV)
        I("tensor", "matmul", pH[:, 0:16], onesrf[0:1, :], rows[0:1, R_C:R_C + 16],
          start=True, stop=True, inc=tT)
        W("scalar", tT)
        I("scalar", "copy", out=cst[:, 0:16], in_=pH[:, 0:16], inc=tA)
        barrier()
        dump("d_aa", aa[:, :])
        dump("d_cst", cst[:, :])
        if DEBUG:
            I("sync", "dma_start", out=dbg["d_rows"][:, :], in_=rows[:, :], inc=dX)

        kv_rel = [0, 0, 0, 0]
        pt_rel = [0, 0, 0, 0]

        def pv_group(tp, c0p, nt, final):
            bufk = tp % 4
            bufp = tp % 4
            for m in range(2):
                for vh in range(2):
                    po = (pA, pB, pC, pD)[2 * m + vh]
                    I("tensor", "matmul", po[:, c0p:512],
                      kv_sb[:, 512 * bufk + 256 + 128 * vh:512 * bufk + 256 + 128 * (vh + 1)],
                      pt_sb[:, 1024 * bufp + 512 * m + c0p:1024 * bufp + 512 * m + 512],
                      start=(tp == 0), stop=(tp == nt - 1))
                pg_s = (pG, pH)[m]
                I("tensor", "matmul", pg_s[0:1, c0p:512], onesc[:, 0:1],
                  pt_sb[:, 1024 * bufp + 512 * m + c0p:1024 * bufp + 512 * m + 512],
                  start=(tp == 0), stop=(tp == nt - 1),
                  inc=(tT if m == 1 else None))
            kv_rel[bufk] = tT.n
            pt_rel[bufp] = tT.n

        def attn_head(gdh, h, qh):
            nt = 16 if qh == 0 else 32
            exp_done = {}
            prev = None
            for t in range(nt):
                jmin = max(t // 4 - 4 * qh, 0)
                c0 = 128 * jmin
                buf = t % 4
                ptb = t % 4
                rt, jt = t % 4, t // 4
                W("sync", tT, kv_rel[buf])
                rbase = 2048 * rt
                I("sync", "dma_start", out=kv_sb[:, 512 * buf:512 * buf + 128],
                  in_=ag_out[rbase + 256 * gdh:rbase + 256 * gdh + 128,
                             128 * jt:128 * (jt + 1)], inc=kvS[buf])
                I("sync", "dma_start", out=kv_sb[:, 512 * buf + 128:512 * buf + 256],
                  in_=ag_out[rbase + 256 * gdh + 128:rbase + 256 * gdh + 256,
                             128 * jt:128 * (jt + 1)], inc=kvS[buf])
                W("gpsimd", tT, kv_rel[buf])
                I("gpsimd", "dma_start", out=kv_sb[:, 512 * buf + 256:512 * buf + 512],
                  in_=ag_out[rbase + 1024 + 128 * jt:rbase + 1024 + 128 * (jt + 1),
                             256 * gdh:256 * (gdh + 1)], inc=kvS[buf])

                W("tensor", kvS[buf])
                masked = (qh == 0) or (t >= 16)
                for m in range(2):
                    ps = (pE, pF)[m]
                    qoff = A_Q + 1024 * (2 * h + m) + 512 * qh
                    if t >= 1:
                        W("tensor", tA, exp_done[(t - 1, m)])
                    I("tensor", "matmul", ps[:, c0:512],
                      kv_sb[:, 512 * buf + 128 * m:512 * buf + 128 * (m + 1)],
                      aa[:, qoff + c0:qoff + 512],
                      start=True, stop=(not masked),
                      inc=(tT if (m == 1 and not masked) else None))
                if masked:
                    for m in range(2):
                        ps = (pE, pF)[m]
                        I("tensor", "matmul", ps[:, c0:c0 + 128], iden[:, :],
                          mask_sb[:, 128 * rt:128 * (rt + 1)],
                          start=False, stop=True,
                          inc=(tT if m == 1 else None))
                scores_done = tT.n

                if prev is not None:
                    pv_group(prev[0], prev[1], nt, False)

                W("scalar", tT, scores_done)
                W("scalar", tT, pt_rel[ptb])
                for m in range(2):
                    ps = (pE, pF)[m]
                    I("scalar", "activation",
                      pt_sb[:, 1024 * ptb + 512 * m + c0:1024 * ptb + 512 * m + 512],
                      ps[:, c0:512], AF.Exp,
                      bias=cst[:, 2 * h + m:2 * h + m + 1], scale=SCALE,
                      inc=tA)
                    exp_done[(t, m)] = tA.n
                prev = (t, c0)

            W("tensor", tA, exp_done[(nt - 1, 1)])
            pv_group(prev[0], prev[1], nt, True)
            pv_done = tT.n

            W("vector", tT, pv_done)
            I("vector", "reciprocal", out=rows[0:1, R_T0:R_T0 + 512], in_=pG[0:1, 0:512])
            I("vector", "reciprocal", out=rows[0:1, R_T0 + 512:R_T0 + 1024],
              in_=pH[0:1, 0:512])
            I("vector", "tensor_scalar", out=rows[0:1, R_T0 + 512:R_T0 + 1024],
              in0=rows[0:1, R_T0 + 512:R_T0 + 1024],
              scalar1=rows[0:1, R_LAM:R_LAM + 1], scalar2=None, op0=ALU.mult, inc=tV)
            rdone = tV.n
            W("tensor", tV, rdone)
            I("tensor", "matmul", pG[:, 0:512], onesrf[0:1, :],
              rows[0:1, R_T0:R_T0 + 512], start=True, stop=True, inc=tT)
            W("scalar", tT)
            I("scalar", "copy", out=tmpf[:, :], in_=pG[:, :], inc=tA)
            W("vector", tA)
            for vh in range(2):
                dst = A_O + 1024 * (2 * h + vh) + 512 * qh
                I("vector", "tensor_mul", aa[:, dst:dst + 512],
                  (pA, pB)[vh][:, :], tmpf[:, :], inc=(tV if vh == 1 else None))
            prod1_done = tV.n
            tmpf2 = tmp_a[:, :].bitcast(F32)
            I("tensor", "matmul", pH[:, 0:512], onesrf[0:1, :],
              rows[0:1, R_T0 + 512:R_T0 + 1024], start=True, stop=True, inc=tT)
            W("scalar", tT)
            I("scalar", "copy", out=tmpf2[:, :], in_=pH[:, :], inc=tA)
            W("vector", tA)
            for vh in range(2):
                pm = (pC, pD)[vh]
                dst = A_O + 1024 * (2 * h + vh) + 512 * qh
                I("vector", "tensor_mul", pm[:, :], pm[:, :], tmpf2[:, :])
                I("vector", "tensor_sub", aa[:, dst:dst + 512],
                  aa[:, dst:dst + 512], pm[:, :], inc=(tV if vh == 1 else None))
            comb_done = tV.n
            W("scalar", tV, comb_done)
            for vh in range(2):
                dst = A_O + 1024 * (2 * h + vh) + 512 * qh
                I("scalar", "activation", tmp_a[:, 512 * vh:512 * (vh + 1)],
                  aa[:, dst:dst + 512], AF.Square, inc=(tA if vh == 1 else None))
            W("tensor", tA)
            for vh in range(2):
                I("tensor", "matmul", pG[0:1, 0:512], onesc[:, 0:1],
                  tmp_a[:, 512 * vh:512 * (vh + 1)], start=(vh == 0), stop=(vh == 1),
                  inc=(tT if vh == 1 else None))
            W("vector", tT)
            I("vector", "tensor_scalar", out=rows[0:1, R_T0:R_T0 + 512],
              in0=pG[0:1, 0:512], scalar1=1.0 / VD, scalar2=EPS,
              op0=ALU.mult, op1=ALU.add)
            I("vector", "reciprocal", out=rows[0:1, R_T0:R_T0 + 512],
              in_=rows[0:1, R_T0:R_T0 + 512], inc=tV)
            W("scalar", tV)
            I("scalar", "activation", rows[0:1, R_T0 + 512:R_T0 + 1024],
              rows[0:1, R_T0:R_T0 + 512], AF.Sqrt, inc=tA)
            W("tensor", tA)
            I("tensor", "matmul", pH[:, 0:512], onesrf[0:1, :],
              rows[0:1, R_T0 + 512:R_T0 + 1024], start=True, stop=True, inc=tT)
            W("scalar", tT)
            I("scalar", "copy", out=tmpf[:, :], in_=pH[:, :], inc=tA)
            W("vector", tA)
            for vh in range(2):
                dst = A_O + 1024 * (2 * h + vh) + 512 * qh
                I("vector", "scalar_tensor_tensor",
                  out=aa[:, dst:dst + 512], in0=aa[:, dst:dst + 512],
                  scalar=cst[:, 16 + vh:17 + vh], in1=tmpf[:, :],
                  op0=ALU.mult, op1=ALU.mult, inc=tV)

        for gdh in range(NKV):
            for hh2 in range(2):
                h = 2 * gdh + hh2
                for qh in range(2):
                    attn_head(gdh, h, qh)
                    barrier(("tensor", "vector", "scalar"))

        dump("d_o", aa[:, A_O:A_O + 16384])

        ores = {}
        for oc in range(16):
            wbuf, ctr, rdy = next_unit([(outw_l[oc, :, 0:1024], 1024),
                                        (outw_l[oc, :, 1024:2048], 1024)])
            W("tensor", ctr, rdy)
            for tb in range(2):
                po = (pA, pC)[tb] if oc % 2 == 0 else (pB, pD)[tb]
                W("tensor", tV, ores.get((oc - 2, tb), 0))
                for ic in range(16):
                    I("tensor", "matmul", po[:, :],
                      wb_sb[:, 2048 * wbuf + 128 * ic:2048 * wbuf + 128 * (ic + 1)],
                      aa[:, A_O + 1024 * ic + 512 * tb:A_O + 1024 * ic + 512 * (tb + 1)],
                      start=(ic == 0), stop=(ic == 15),
                      inc=(tT if ic == 15 else None))
                W("vector", tT)
                I("vector", "scalar_tensor_tensor",
                  out=x_sb[:, 1024 * oc + 512 * tb:1024 * oc + 512 * (tb + 1)],
                  in0=po[:, :], scalar=cst[:, 48 + oc:49 + oc],
                  in1=x_sb[:, 1024 * oc + 512 * tb:1024 * oc + 512 * (tb + 1)],
                  op0=ALU.add, op1=ALU.add, inc=tV)
                ores[(oc, tb)] = tV.n
            release_unit(wbuf)
        barrier()
        dump("d_x2", x_sb[:, :])

        layernorm(to_fp8=True)
        barrier()
        res3 = ffn()

        for oc in range(16):
            W("sync", tV, res3[(oc, 1, 1)])
            I("sync", "dma_start",
              out=xout[128 * oc:128 * (oc + 1), :],
              in_=x_sb[:, 1024 * oc:1024 * (oc + 1)], inc=dX)
        W("sync", dX)

        block = ctx.enter_context(nc.Block())

        @block.sync
        def _(e):
            for f in prog["sync"]:
                f(e)

        @block.gpsimd
        def _(e):
            for f in prog["gpsimd"]:
                f(e)

        @block.vector
        def _(e):
            for f in prog["vector"]:
                f(e)

        @block.scalar
        def _(e):
            for f in prog["scalar"]:
                f(e)

        @block.tensor
        def _(e):
            for f in prog["tensor"]:
                f(e)

    return nc


_NC_CACHE = None


def kernel(L0_attn_hidden_states, L0_attn_Wqkv_w, L0_attn_Wqkv_b,
           L0_attn_lambda_q1, L0_attn_lambda_k1, L0_attn_lambda_q2,
           L0_attn_lambda_k2, L0_attn_subln_w, L0_attn_out_w, L0_attn_out_b,
           L0_ffn_w_gating, L0_ffn_w_linear):
    global _NC_CACHE
    x0 = np.asarray(L0_attn_hidden_states, dtype=np.float32)
    Wqkv = np.asarray(L0_attn_Wqkv_w, dtype=np.float32)
    bqkv = np.asarray(L0_attn_Wqkv_b, dtype=np.float32)
    outw = np.asarray(L0_attn_out_w, dtype=np.float32)
    outb = np.asarray(L0_attn_out_b, dtype=np.float32)
    wg = np.asarray(L0_ffn_w_gating, dtype=np.float32)
    wl = np.asarray(L0_ffn_w_linear, dtype=np.float32)
    subln = np.asarray(L0_attn_subln_w, dtype=np.float32)

    wqkv_l = np.ascontiguousarray(
        Wqkv[:3072].reshape(24, 128, 16, 128).transpose(0, 3, 2, 1).reshape(24, 128, 2048))
    wv_r = np.ascontiguousarray(Wqkv[3072:].reshape(1024, 16, 128).transpose(1, 2, 0))
    outw_l = np.ascontiguousarray(
        outw.reshape(16, 128, 16, 128).transpose(0, 3, 2, 1).reshape(16, 128, 2048))
    wg_l = np.ascontiguousarray(
        wg.reshape(2, 16, 128, 64, 128).transpose(0, 3, 2, 1, 4).reshape(2, 64, 128, 2048))
    wl_l = np.ascontiguousarray(
        wl.reshape(64, 128, 16, 128).transpose(2, 1, 0, 3).reshape(16, 128, 8192))
    bq_cols = np.ascontiguousarray(bqkv[:3072].reshape(24, 128).T)
    bqv_row = np.ascontiguousarray(bqkv[3072:].reshape(1, 1024))
    outb_cols = np.ascontiguousarray(outb.reshape(16, 128).T)
    subln_cols = np.ascontiguousarray(subln.reshape(2, 128).T)
    lam_row = np.ascontiguousarray(np.concatenate(
        [np.asarray(L0_attn_lambda_q1, dtype=np.float32),
         np.asarray(L0_attn_lambda_k1, dtype=np.float32),
         np.asarray(L0_attn_lambda_q2, dtype=np.float32),
         np.asarray(L0_attn_lambda_k2, dtype=np.float32)]).reshape(1, 512))

    shared = dict(wqkv_l=wqkv_l, wv_r=wv_r, outw_l=outw_l, wg_l=wg_l, wl_l=wl_l,
                  bq_cols=bq_cols, bqv_row=bqv_row, outb_cols=outb_cols,
                  subln_cols=subln_cols, lam_row=lam_row)

    tok_idx = {}
    in_maps = []
    kl = np.arange(128)[:, None]
    ql = np.arange(128)[None, :]
    for c in range(8):
        b, r = c // 4, c % 4
        idx = np.concatenate([np.arange(128 * (4 * j + r), 128 * (4 * j + r) + 128)
                              for j in range(8)])
        tok_idx[c] = (b, idx)
        mask = np.zeros((128, 512), dtype=np.float32)
        for rr in range(4):
            mask[:, 128 * rr:128 * (rr + 1)] = np.where(
                128 * rr + kl > 128 * r + ql, NEG, 0.0)
        m = dict(shared)
        m["xin"] = np.ascontiguousarray(x0[b, idx, :].T)
        m["mask_in"] = mask
        m["iden_in"] = np.eye(128, dtype=np.float32)
        in_maps.append(m)

    if _NC_CACHE is None:
        _NC_CACHE = build_program()
    res = run_bass_kernel_spmd(_NC_CACHE, in_maps, core_ids=list(range(8)))
    globals()["_LAST_RES"] = res

    out = np.empty((B, S, DM), dtype=np.float32)
    for c in range(8):
        b, idx = tok_idx[c]
        out[b, idx, :] = res.results[c]["xout"].T
    return out


if __name__ == "__main__":
    import reference
    inputs = reference.setup_inputs(0)
    inputs = {k: np.asarray(v) for k, v in inputs.items()}
    got = kernel(**inputs)
    exp = np.asarray(reference.reference(**inputs))
    err = np.mean((got - exp) ** 2) / np.mean(exp ** 2)
    print("residual variance ratio:", err)
    print("max abs diff:", np.abs(got - exp).max())
